# version 04
# baseline (speedup 1.0000x reference)
import contextlib
import os
import numpy as np
import concourse.bass as bass
import concourse.mybir as mybir
from concourse.bass_utils import run_bass_kernel_spmd

F32 = mybir.dt.float32
BF16 = mybir.dt.bfloat16
I32 = mybir.dt.int32
AF = mybir.ActivationFunctionType
ALU = mybir.AluOpType
AX = mybir.AxisListType

PI = float(np.pi)
INV2PI = float(1.0 / (2.0 * np.pi))
D = 2048
S = 2048
B = 4
TOWN = 1024
NIC = 16
E = 8
EPS = 1e-5
STAGE = os.environ.get("KSTAGE", "full")

NLANES = {"sync": 8, "gpsimd": 8, "scalar": 4}


class _Stream:
    __slots__ = ("name", "sem_c", "c", "lanes", "lane_tot", "lane_rr", "items", "known")

    def __init__(self, name, sem_c, lanes):
        self.name = name
        self.sem_c = sem_c
        self.c = 0
        self.lanes = lanes
        self.lane_tot = [0] * len(lanes)
        self.lane_rr = 0
        self.items = []
        self.known = {}


class Sched:
    ENGINES = ("tensor", "vector", "scalar", "gpsimd", "sync")

    def __init__(self, nc, stack):
        self.nc = nc
        self.streams = {}
        for name in self.ENGINES:
            sem_c = None if name == "sync" else stack.enter_context(
                nc.semaphore(f"semc_{name}"))
            lanes = [stack.enter_context(nc.semaphore(f"semd_{name}{i}"))
                     for i in range(NLANES.get(name, 0))]
            self.streams[name] = _Stream(name, sem_c, lanes)
        self.last_writer = {}
        self.readers = {}
        self.n_ops = 0
        self.n_waits = 0

    def _emit_waits(self, me, deps):
        for sid in sorted(deps):
            sem, val = deps[sid]
            if me.known.get(sid, 0) >= val:
                continue
            me.items.append(("wait", sem, val))
            me.known[sid] = val
            self.n_waits += 1

    def _gather_deps(self, reads, writes):
        deps = {}

        def add(sem, val):
            sid = id(sem)
            if sid not in deps or deps[sid][1] < val:
                deps[sid] = (sem, val)

        for k in reads:
            lw = self.last_writer.get(k)
            if lw is not None:
                add(lw[1], lw[2])
        for k in writes:
            lw = self.last_writer.get(k)
            if lw is not None:
                add(lw[1], lw[2])
            for (sem, val) in self.readers.get(k, {}).values():
                add(sem, val)
        return deps

    def _record(self, reads, writes, dep):
        for k in writes:
            self.last_writer[k] = dep
            self.readers[k] = {}
        for k in reads:
            self.readers.setdefault(k, {})[(dep[0], id(dep[1]))] = (dep[1], dep[2])
        self.n_ops += 1

    def op(self, stream, fn, reads=(), writes=()):
        me = self.streams[stream]
        self._emit_waits(me, self._gather_deps(reads, writes))
        me.c += 1
        me.items.append(("op", fn, me.sem_c, 1))
        self._record(reads, writes, (stream, me.sem_c, me.c))

    def dma(self, stream, fn, reads=(), writes=()):
        me = self.streams[stream]
        lane = me.lane_rr
        me.lane_rr = (me.lane_rr + 1) % len(me.lanes)
        deps = self._gather_deps(reads, writes)
        sem = me.lanes[lane]
        prev = me.lane_tot[lane]
        if prev and me.known.get(id(sem), 0) < prev:
            if id(sem) not in deps or deps[id(sem)][1] < prev:
                deps[id(sem)] = (sem, prev)
        self._emit_waits(me, deps)
        me.lane_tot[lane] += 16
        me.items.append(("op", fn, sem, 16))
        self._record(reads, writes, (stream, sem, me.lane_tot[lane]))

    def wait_all_dmas(self, stream):
        me = self.streams[stream]
        deps = {}
        for st in self.streams.values():
            for i, tot in enumerate(st.lane_tot):
                if tot:
                    deps[id(st.lanes[i])] = (st.lanes[i], tot)
        self._emit_waits(me, deps)

    def finalize(self, blk):
        for name in self.ENGINES:
            st = self.streams[name]
            if not st.items:
                continue

            def body(eng, st=st):
                for item in st.items:
                    if item[0] == "wait":
                        eng.wait_ge(item[1], item[2])
                    else:
                        _, fn, sem, inc = item
                        ins = fn(eng)
                        assert ins is not None, f"op on {st.name} returned None"
                        ins.then_inc(sem, inc)
            getattr(blk, name)(body)


class Arena:

    def __init__(self, gran=1):
        self.live = {}
        self.gran = gran

    def claim(self, key, lo, hi):
        g = self.gran
        lo = (lo // g) * g
        hi = ((hi + g - 1) // g) * g
        extra = []
        for k, (a, b) in list(self.live.items()):
            if k != key and a < hi and lo < b:
                extra.append(k)
                if lo <= a and b <= hi:
                    del self.live[k]
        cur = self.live.get(key)
        if cur:
            lo, hi = min(lo, cur[0]), max(hi, cur[1])
        self.live[key] = (lo, hi)
        return extra



def _wimg(WT, obs):
    K, O = WT.shape
    return np.ascontiguousarray(
        WT.reshape(K // 128, 128, O // obs, obs).transpose(2, 1, 0, 3))


def _prep(inputs):
    f32 = lambda a: np.ascontiguousarray(np.asarray(a, dtype=np.float32))
    x = f32(inputs["L0_attn_hidden_states"])
    Wq = f32(inputs["L0_attn_Wq"])
    bq = f32(inputs["L0_attn_bq"])
    Wk = f32(inputs["L0_attn_Wk"])
    bk = f32(inputs["L0_attn_bk"])
    Wv = f32(inputs["L0_attn_Wv"])
    bv = f32(inputs["L0_attn_bv"])
    Wo = f32(inputs["L0_attn_Wo"])
    bo = f32(inputs["L0_attn_bo"])
    R = f32(inputs["L0_attn_rand_matrix"])
    ebias = f32(inputs["L0_ffn_expert_bias"])
    gw = f32(inputs["L0_ffn_gate_w"])
    w1 = f32(inputs["L0_ffn_w1"])
    w2 = f32(inputs["L0_ffn_w2"])
    w3 = f32(inputs["L0_ffn_w3"])
    sw1 = f32(inputs["L0_ffn_sw1"])
    sw2 = f32(inputs["L0_ffn_sw2"])
    sw3 = f32(inputs["L0_ffn_sw3"])
    c = f32(inputs["L0_c"])
    wmod = f32(inputs["L0_wmod"])

    shared = {}
    shared["wkt"] = _wimg(Wk.T.copy(), 512)
    shared["wvt"] = _wimg(Wv.T.copy(), 512)
    shared["wot"] = _wimg(Wo.T.copy(), 512)
    shared["wmod"] = _wimg(wmod, 512)
    w1i = np.empty((E, 8, 128, NIC, 128), np.float32)
    w3i = np.empty((E, 8, 128, NIC, 128), np.float32)
    w2i = np.empty((E, 8, 128, D), np.float32)
    for e in range(E):
        w1i[e] = _wimg(w1[e].T.copy(), 128)
        w3i[e] = _wimg(w3[e].T.copy(), 128)
        w2i[e] = w2[e].T.reshape(8, 128, D)
    shared["w1i"] = w1i
    shared["w3i"] = w3i
    shared["w2i"] = w2i
    shared["sw1i"] = _wimg(sw1.T.copy(), 128)
    shared["sw3i"] = _wimg(sw3.T.copy(), 128)
    shared["sw2i"] = np.ascontiguousarray(sw2.T.reshape(8, 128, D))
    shared["gwt"] = np.ascontiguousarray(
        gw.T.reshape(NIC, 128, 8).transpose(1, 0, 2)).reshape(128, 128)
    shared["ebias"] = ebias.reshape(1, 8)
    shared["rt"] = R.T.copy()
    shared["bk"] = bk.reshape(1, 1024)
    shared["bv"] = bv.reshape(1, 1024)
    shared["bo"] = bo.reshape(1, 2048)
    inv_freq = 1.0 / (10000.0 ** (np.arange(0, 128, 2, dtype=np.float64) / 128))
    pos = np.arange(S, dtype=np.float64)[:, None] * inv_freq[None, :]
    emb = np.concatenate([pos, pos], axis=1)
    shared["cost"] = np.ascontiguousarray(
        np.cos(pos).astype(np.float32).reshape(16, 128, 64).transpose(1, 0, 2))
    shared["sint"] = np.ascontiguousarray(
        np.sin(pos).astype(np.float32).reshape(16, 128, 64).transpose(1, 0, 2))
    shared["ident"] = np.eye(128, dtype=np.float32)
    am = np.zeros((128, 128), np.float32)
    for j in range(16):
        for hk in range(8):
            am[j * 8 + hk].reshape(8, 16)[:, j] = 8.0 / 128.0
    shared["amask"] = am

    in_maps = []
    for core in range(8):
        b, H = core // 2, core % 2
        m = dict(shared)
        m["xin"] = x[b]
        m["xown"] = np.ascontiguousarray(x[b][H * 1024:(H + 1) * 1024])
        m["crowp"] = np.ascontiguousarray(c[b].reshape(NIC, 128).T)
        m["wqt"] = _wimg(Wq.T[:, H * 1024:(H + 1) * 1024].copy(), 512)
        m["bq"] = np.ascontiguousarray(bq[H * 1024:(H + 1) * 1024].reshape(1, 1024))
        in_maps.append(m)
    return in_maps



def _build(stage):
    nc = bass.Bass("TRN2", target_bir_lowering=False, debug=False)
    dram = {}
    for name, shape in [
        ("xin", [S, D]), ("xown", [TOWN, D]), ("crowp", [128, 16]),
        ("wqt", [2, 128, NIC, 512]), ("bq", [1, 1024]),
        ("wkt", [2, 128, NIC, 512]), ("bk", [1, 1024]),
        ("wvt", [2, 128, NIC, 512]), ("bv", [1, 1024]),
        ("wot", [4, 128, NIC, 512]), ("bo", [1, 2048]),
        ("wmod", [24, 128, NIC, 512]), ("rt", [128, 64]),
        ("w1i", [E, 8, 128, NIC, 128]), ("w3i", [E, 8, 128, NIC, 128]),
        ("w2i", [E, 8, 128, D]),
        ("sw1i", [8, 128, NIC, 128]), ("sw3i", [8, 128, NIC, 128]),
        ("sw2i", [8, 128, D]),
        ("gwt", [128, 128]), ("ebias", [1, 8]),
        ("cost", [128, 16, 64]), ("sint", [128, 16, 64]),
        ("ident", [128, 128]), ("amask", [128, 128]),
    ]:
        dram[name] = nc.dram_tensor(name, shape, F32, kind="ExternalInput").ap()
    out = nc.dram_tensor("out", [TOWN, D], F32, kind="ExternalOutput").ap()
    dmod = nc.dram_tensor("dmod", [6, D], F32, kind="Internal").ap()
    sdram = nc.dram_tensor("sdram", [E, TOWN], BF16, kind="Internal").ap()
    w2scr = nc.dram_tensor("w2scr", [9, 8, 128, D], BF16, kind="Internal").ap()
    dbg = None
    if stage != "full":
        dbg = nc.dram_tensor("dbg", [128, 16384], F32, kind="ExternalOutput").ap()

    with contextlib.ExitStack() as st:
        AX1 = st.enter_context(nc.sbuf_tensor("AX1", [128, 16384], F32))
        AHT = st.enter_context(nc.sbuf_tensor("AHT", [128, 16384], BF16))
        AOT = st.enter_context(nc.sbuf_tensor("AOT", [128, 16384], BF16))
        AMS = st.enter_context(nc.sbuf_tensor("AMS", [128, 16384], BF16))
        AW = st.enter_context(nc.sbuf_tensor("AW", [128, 16384], BF16))
        brows = st.enter_context(nc.sbuf_tensor("brows", [1, 1024], BF16))
        identb = st.enter_context(nc.sbuf_tensor("identb", [128, 128], BF16))
        identf = st.enter_context(nc.sbuf_tensor("identf", [128, 128], F32))
        rtc = st.enter_context(nc.sbuf_tensor("rtc", [128, 64], BF16))
        rtf = st.enter_context(nc.sbuf_tensor("rtf", [128, 64], F32))
        amaskb = st.enter_context(nc.sbuf_tensor("amaskb", [128, 128], F32))
        onesr = st.enter_context(nc.sbuf_tensor("onesr", [1, 128], BF16))
        scolf = st.enter_context(nc.sbuf_tensor("scolf", [128, 16], F32))
        scol = st.enter_context(nc.sbuf_tensor("scol", [128, 16], BF16))
        cacol = st.enter_context(nc.sbuf_tensor("cacol", [128, 16], F32))
        sacol = st.enter_context(nc.sbuf_tensor("sacol", [128, 16], F32))
        cbcol = st.enter_context(nc.sbuf_tensor("cbcol", [128, 16], F32))
        sbcol = st.enter_context(nc.sbuf_tensor("sbcol", [128, 16], F32))
        cosbuf = st.enter_context(nc.sbuf_tensor("cosbuf", [128, 1024], BF16))
        sinbuf = st.enter_context(nc.sbuf_tensor("sinbuf", [128, 1024], BF16))
        modr = st.enter_context(nc.sbuf_tensor("modr", [1, 1024], F32))
        n2q = st.enter_context(nc.sbuf_tensor("n2q", [128, 16], F32))
        n2k = st.enter_context(nc.sbuf_tensor("n2k", [128, 16], F32))
        invq = st.enter_context(nc.sbuf_tensor("invq", [128, 16], F32))
        invk = st.enter_context(nc.sbuf_tensor("invk", [128, 16], F32))
        lns = st.enter_context(nc.sbuf_tensor("lns", [128, 8], F32))
        gsc = st.enter_context(nc.sbuf_tensor("gsc", [128, 64], F32))
        gtmp = st.enter_context(nc.sbuf_tensor("gtmp", [128, 32], F32))
        gm = st.enter_context(nc.sbuf_tensor("gm", [128, 2], F32))
        sw_t = st.enter_context(nc.sbuf_tensor("sw_t", [128, 64], F32))
        ebc = st.enter_context(nc.sbuf_tensor("ebc", [128, 8], F32))
        cpi = st.enter_context(nc.sbuf_tensor("cpi", [128, 3], F32))
        pb0 = st.enter_context(nc.psum_tensor("pb0", [128, 2048], F32))
        pb1 = st.enter_context(nc.psum_tensor("pb1", [128, 2048], F32))
        blk = st.enter_context(nc.Block())
        sch = Sched(nc, st)

        ar_x1 = Arena()
        ar_ht = Arena()
        ar_ot = Arena()
        ar_ms = Arena()
        ar_aw = Arena()
        ar_p0 = Arena(gran=2048)
        ar_p1 = Arena(gran=2048)

        ax1f = AX1[:]
        ax1b = AX1[:].bitcast(BF16)
        ahtb = AHT[:]
        ahtf = AHT[:].bitcast(F32)
        aotb = AOT[:]
        amsb = AMS[:]
        amsf = AMS[:].bitcast(F32)
        amsi = AMS[:].bitcast(I32)
        awb = AW[:]
        pb0b = pb0[:].bitcast(BF16)
        pb1b = pb1[:].bitcast(BF16)

        def W(arena, key, lo, hi, extra=()):
            return [key] + arena.claim(key, lo, hi) + list(extra)

        qnT = amsb[:, 0:1024]
        knT = amsb[:, 1024:2048]
        qpt = amsb[:, 2048:3072]
        kpt = amsb[:, 3072:4096]
        v16 = amsb[:, 4096:5120]
        a16 = [amsb[:, 5120:5248], amsb[:, 5248:5376]]
        ti32 = amsi[:, 2688:3200]
        wabs = amsf[:, 2688:3200]
        kfb = amsf[:, 3200:3712]
        wfr = amsf[:, 3712:4224]
        rtmp1 = amsf[:, 3712:3968]
        rtmp2 = amsf[:, 3968:4224]
        s1b = amsb[:, 8448:8960]
        c1b = amsb[:, 8960:9472]
        sq1b = amsb[:, 9472:9984]
        srb = amsb[:, 9984:10496]
        sqq = amsb[:, 10496:11520]
        sqk = amsb[:, 11520:12544]
        qparr = amsb[:, 12544:13568]
        kparr = amsb[:, 13568:14592]
        lnxsB = amsf[:, 4096:6144]
        wo_buf = amsb
        sbc = [amsf[:, 0:1024], amsf[:, 1024:2048]]
        gbc = amsf[:, 2048:4096]
        scrA = amsf[:, 4096:5120]
        scrB = amsf[:, 5120:6144]
        scr2 = amsf[:, 6144:8192]
        sT = amsb[0:8, 12288:13312]
        gac = ahtf[:, 0:2048]
        evt = [ahtf[:, 2048:2560], ahtf[:, 2560:3072]]

        x1t = lambda t: ax1f[:, t * 2048:(t + 1) * 2048]
        qr_t = lambda t: ax1b[:, t * 1024:(t + 1) * 1024]
        kr_t = lambda t: ax1b[:, 8192 + t * 1024:8192 + (t + 1) * 1024]
        vb_t = lambda t: ax1b[:, 16384 + t * 1024:16384 + (t + 1) * 1024]
        xstage = ax1f[:, 12288:14336]
        lnxsA = ax1f[:, 14336:16384]
        hT_tile = lambda ic, t: ahtb[:, ic * 1024 + t * 128: ic * 1024 + (t + 1) * 128]
        outT = aotb
        heA, heB = aotb[:, 0:8192], aotb[:, 8192:16384]
        slot = lambda i: awb[:, i * 2048:(i + 1) * 2048]
        blkbuf = lambda i: awb[:, i * 8192:(i + 1) * 8192]


        sch.dma("gpsimd", lambda g: g.dma_start(identb[:], dram["ident"]), writes=["identb"])
        sch.dma("sync", lambda s: s.dma_start(identf[:], dram["ident"]), writes=["identf"])
        sch.dma("sync", lambda s: s.dma_start(rtf[:], dram["rt"]), writes=["rtf"])
        sch.op("vector", lambda v: v.tensor_scalar_mul(rtc[:], rtf[:], INV2PI),
               reads=["rtf"], writes=["rtc"])
        sch.dma("sync", lambda s: s.dma_start(amaskb[:], dram["amask"]), writes=["amaskb"])
        sch.op("vector", lambda v: v.memset(onesr[:], 1.0), writes=["onesr"])
        sch.op("vector", lambda v: v.memset(cpi[:, 0:1], PI), writes=["cpi"])
        sch.op("vector", lambda v: v.memset(cpi[:, 1:2], -PI), writes=["cpi"])
        sch.op("vector", lambda v: v.memset(cpi[:, 2:3], PI / 2), writes=["cpi"])
        sch.dma("sync", lambda s: s.dma_start(scolf[:], dram["crowp"]), writes=["scolf"])
        sch.op("scalar", lambda s: s.activation(scol[:], scolf[:], AF.Silu),
               reads=["scolf"], writes=["scol"])

        sch.dma("gpsimd", lambda g: g.dma_start(ebc[:], dram["ebias"].broadcast_to([128, 8])),
                writes=["ebc"])
        sch.dma("gpsimd", lambda g: g.dma_start(
            cosbuf[:].rearrange("p (a b) -> p a b", a=16), dram["cost"]), writes=["cosbuf"])
        sch.dma("gpsimd", lambda g: g.dma_start(
            sinbuf[:].rearrange("p (a b) -> p a b", a=16), dram["sint"]), writes=["sinbuf"])

        pmod = pb1[0:1, 1536:2048]
        cols4 = {0: sacol, 1: cacol, 3: sbcol, 4: cbcol}

        def emit_mod_jb(jb):
            ws = jb % 2
            wkey = f"awblk{ws}"
            sch.dma("gpsimd", lambda g, jb=jb, ws=ws: g.dma_start(
                blkbuf(ws)[:].rearrange("p (c n) -> p c n", c=NIC), dram["wmod"][jb]),
                writes=W(ar_aw, wkey, ws * 16384, (ws + 1) * 16384))

            def modmm(t, ws=ws):
                for kc in range(NIC):
                    ins = t.matmul(pmod, scol[:, kc:kc + 1],
                                   blkbuf(ws)[:, kc * 512:(kc + 1) * 512],
                                   start=(kc == 0), stop=(kc == NIC - 1))
                return ins
            sch.op("tensor", modmm, reads=[wkey, "scol"],
                   writes=W(ar_p1, "pmod", 6144, 8192))
            v_idx, q_idx = jb // 4, jb % 4
            mslot = jb % 2
            mr = modr[0:1, mslot * 512:(mslot + 1) * 512]
            mkey = f"modr{mslot}"
            if v_idx in (1, 4):
                sch.op("vector", lambda v, mr=mr: v.tensor_scalar_add(mr, pmod, 1.0),
                       reads=["pmod"], writes=[mkey])
            elif v_idx in (2, 5):
                t1slot = 1 - mslot
                mt1 = modr[0:1, t1slot * 512:(t1slot + 1) * 512]
                mt1key = f"modr{t1slot}"
                sch.op("scalar", lambda s, mr=mr: s.activation(mr, pmod, AF.Sign),
                       reads=["pmod"], writes=[mkey])
                sch.op("scalar", lambda s, mt1=mt1: s.activation(mt1, pmod, AF.Abs),
                       reads=["pmod"], writes=[mt1key])
                sch.op("vector", lambda v, mt1=mt1: v.tensor_scalar_max(mt1, mt1, 0.25),
                       reads=[mt1key], writes=[mt1key])
                sch.op("vector", lambda v, mr=mr, mt1=mt1: v.tensor_mul(mr, mr, mt1),
                       reads=[mkey, mt1key], writes=[mkey])
            else:
                sch.op("vector", lambda v, mr=mr: v.tensor_copy(mr, pmod),
                       reads=["pmod"], writes=[mkey])
            sch.dma("scalar", lambda s, v_idx=v_idx, q_idx=q_idx, mr=mr: s.dma_start(
                dmod[v_idx:v_idx + 1, q_idx * 512:(q_idx + 1) * 512], mr),
                reads=[mkey], writes=[f"dmod{v_idx}"])
            if v_idx in cols4:
                colt = cols4[v_idx]
                ckey = f"col{v_idx}"
                for i in range(4):
                    pvt = pb1[:, 1024 + 64 * (i % 2):1024 + 64 * (i % 2) + 1]
                    sch.op("tensor", lambda tt, mr=mr, i=i, pvt=pvt: tt.transpose(
                        pvt, mr[0:1, i * 128:(i + 1) * 128], identf[0:1, 0:1]),
                        reads=[mkey, "identf"], writes=W(ar_p1, f"pcol{i % 2}", 4096, 6144))
                    sch.op("vector", lambda v, colt=colt, q_idx=q_idx, i=i, pvt=pvt:
                           v.tensor_copy(colt[:, q_idx * 4 + i:q_idx * 4 + i + 1], pvt),
                           reads=[f"pcol{i % 2}"], writes=[ckey])

        for jb in range(8):
            emit_mod_jb(jb)
        if stage == "mod":
            for jb in range(8, 24):
                emit_mod_jb(jb)

        if stage == "mod":
            sch.dma("sync", lambda s: s.dma_start(dbg[0:6, 0:2048], dmod),
                    reads=[f"dmod{v}" for v in range(6)])
            sch.wait_all_dmas("sync")
            sch.finalize(blk)
            return nc

        def ln_tile(src_ap, src_keys, lnxs_ap, lnxs_w):
            sch.op("vector", lambda v: v.reduce_sum(lns[:, 0:1], src_ap, axis=AX.X),
                   reads=src_keys, writes=["lns0"])
            sch.op("scalar", lambda s: s.activation(lnxs_ap, src_ap, AF.Square),
                   reads=src_keys, writes=lnxs_w())
            sch.op("vector", lambda v: v.reduce_sum(lns[:, 1:2], lnxs_ap, axis=AX.X),
                   reads=["lnxs"], writes=["lns1"])
            sch.op("vector", lambda v: v.tensor_scalar_mul(lns[:, 2:3], lns[:, 0:1], 1.0 / D),
                   reads=["lns0"], writes=["lns2"])
            sch.op("vector", lambda v: v.tensor_mul(lns[:, 3:4], lns[:, 2:3], lns[:, 2:3]),
                   reads=["lns2"], writes=["lns3"])
            sch.op("vector", lambda v: v.scalar_tensor_tensor(
                lns[:, 4:5], lns[:, 1:2], 1.0 / D, lns[:, 3:4], ALU.mult, ALU.subtract),
                reads=["lns1", "lns3"], writes=["lns4"])
            sch.op("vector", lambda v: v.tensor_scalar_add(lns[:, 4:5], lns[:, 4:5], EPS),
                   reads=["lns4"], writes=["lns4"])
            sch.op("scalar", lambda s: s.activation(lns[:, 5:6], lns[:, 4:5], AF.Sqrt),
                   reads=["lns4"], writes=["lns5"])
            sch.op("vector", lambda v: v.reciprocal(lns[:, 6:7], lns[:, 5:6]),
                   reads=["lns5"], writes=["lns6"])
            sch.op("vector", lambda v: v.scalar_tensor_tensor(
                lns[:, 7:8], lns[:, 2:3], -1.0, lns[:, 6:7], ALU.mult, ALU.mult),
                reads=["lns2", "lns6"], writes=["lns7"])
            sch.op("scalar", lambda s: s.activation(
                lnxs_ap, src_ap, AF.Identity, bias=lns[:, 7:8], scale=lns[:, 6:7]),
                reads=src_keys + ["lns6", "lns7"], writes=lnxs_w())

        def transpose_h_tile(t, lnxs_ap, ccol, scol_, ckeys):
            for ic in range(NIC):
                pcol = (ic % 2) * 512
                pv = pb0[:, pcol:pcol + 128]
                pk = f"ptr{ic % 2}"
                sch.op("tensor", lambda tt, ic=ic, pv=pv: tt.transpose(
                    pv, lnxs_ap[:, ic * 128:(ic + 1) * 128], identf[:]),
                    reads=["lnxs", "identf"],
                    writes=W(ar_p0, pk, pcol * 4, pcol * 4 + 2048))
                sch.op("vector", lambda v, ic=ic, pv=pv, t=t, ccol=ccol, scol_=scol_:
                       v.tensor_scalar(hT_tile(ic, t), pv, ccol[:, ic:ic + 1],
                                       scol_[:, ic:ic + 1], ALU.mult, ALU.add),
                       reads=[pk] + ckeys,
                       writes=W(ar_ht, f"hT{ic}_{t}",
                                ic * 2048 + t * 256, ic * 2048 + (t + 1) * 256))

        wsrc = [("wqt", 0), ("wqt", 1), ("wkt", 0), ("wkt", 1), ("wvt", 0), ("wvt", 1)]

        def emit_ln1_tile(sh, t, src_fn, src_keys_fn, lnxs_ap, lnxs_w, ccol, scol_, ckeys):
            if src_fn is not None:
                tg = sh * 8 + t
                sch.dma("sync", lambda s, tg=tg: s.dma_start(
                    xstage, dram["xin"][tg * 128:(tg + 1) * 128, :]),
                    writes=W(ar_x1, "xstage", 49152, 57344))
                src_ap, src_keys = xstage, ["xstage"]
            else:
                src_ap, src_keys = src_keys_fn
            ln_tile(src_ap, src_keys, lnxs_ap, lnxs_w)
            trb = pb0 if t % 2 == 0 else pb1
            trn = "pb0" if t % 2 == 0 else "pb1"
            tra = ar_p0 if t % 2 == 0 else ar_p1
            for ic in range(NIC):
                pcol = (ic % 2) * 512
                pv = trb[:, pcol:pcol + 128]
                pk = f"ptr{trn}{ic % 2}"
                sch.op("tensor", lambda tt, ic=ic, pv=pv, lnxs_ap=lnxs_ap: tt.transpose(
                    pv, lnxs_ap[:, ic * 128:(ic + 1) * 128], identf[:]),
                    reads=["lnxs", "identf"],
                    writes=W(tra, pk, pcol * 4, pcol * 4 + 2048))
                sch.op("vector", lambda v, ic=ic, pv=pv, t=t, ccol=ccol, scol_=scol_:
                       v.tensor_scalar(hT_tile(ic, t), pv, ccol[:, ic:ic + 1],
                                       scol_[:, ic:ic + 1], ALU.mult, ALU.add),
                       reads=[pk] + ckeys,
                       writes=W(ar_ht, f"hT{ic}_{t}",
                                ic * 2048 + t * 256, ic * 2048 + (t + 1) * 256))

        def emit_qkv_block(sh, ob):
            wname, wbi = wsrc[ob]
            wkey = f"awblk{ob % 2}"
            sch.dma("gpsimd", lambda g, wname=wname, wbi=wbi, ob=ob: g.dma_start(
                blkbuf(ob % 2)[:].rearrange("p (c n) -> p c n", c=NIC),
                dram[wname][wbi]),
                writes=W(ar_aw, wkey, (ob % 2) * 16384, (ob % 2 + 1) * 16384))
            isq, isv = ob < 2, ob >= 4
            bsrc = ["bq", "bq", "bk", "bk", "bv", "bv"][ob]
            brow = brows[0:1, (ob % 2) * 512:(ob % 2) * 512 + 512]
            brkey = f"brow{ob % 2}"
            sch.dma("gpsimd", lambda g, bsrc=bsrc, ob=ob, brow=brow: g.dma_start(
                brow, dram[bsrc][0:1, (ob % 2) * 512:(ob % 2) * 512 + 512]),
                writes=[brkey])
            for t in range(8):
                tg = sh * 8 + t
                pcol = (t % 2) * 512
                pv = pb0[:, pcol:pcol + 512]
                pk = f"pqkv{t % 2}"
                hkeys = [f"hT{ic}_{t}" for ic in range(NIC)]

                def qkvmm(tt, wkey=wkey, t=t, pv=pv, brow=brow, ob=ob):
                    for ic in range(NIC):
                        tt.matmul(pv, hT_tile(ic, t),
                                  blkbuf(ob % 2)[:, ic * 512:(ic + 1) * 512],
                                  start=(ic == 0), stop=False)
                    return tt.matmul(pv, onesr[:], brow, start=False, stop=True)
                sch.op("tensor", qkvmm, reads=[wkey, brkey, "onesr"] + hkeys,
                       writes=W(ar_p0, pk, pcol * 4, pcol * 4 + 2048))
                if isv:
                    dst = vb_t(t)[:, (ob - 4) * 512:(ob - 4) * 512 + 512]
                    sch.op("scalar", lambda s, pv=pv, dst=dst: s.copy(dst, pv),
                           reads=[pk],
                           writes=W(ar_x1, f"vb{sh}_{t}",
                                    32768 + t * 2048, 32768 + (t + 1) * 2048))
                else:
                    srb_ = srb if t % 2 == 0 else amsb[0:128, 14592:15104]
                    srbk = "srb" + ("A" if t % 2 == 0 else "B")
                    srb_lo = 19968 if t % 2 == 0 else 29184
                    sch.op("scalar", lambda s, pv=pv, srb_=srb_: s.copy(srb_[:], pv),
                           reads=[pk], writes=W(ar_ms, srbk, srb_lo, srb_lo + 1024))
                    dstbuf = qr_t(t) if isq else kr_t(t)
                    dkey = (f"qr{sh}_{t}" if isq else f"kr{sh}_{t}")
                    dlo = (0 if isq else 16384) + t * 2048
                    hoff = (ob % 2) * 512
                    srb3 = srb_[:].rearrange("p (h d) -> p h d", h=4)
                    dst3 = dstbuf[:, hoff:hoff + 512].rearrange("p (h d) -> p h d", h=4)
                    r13 = rtmp1[:, 0:256].rearrange("p (h d) -> p h d", h=4)
                    r23 = rtmp2[:, 0:256].rearrange("p (h d) -> p h d", h=4)
                    ca_ = cosbuf[:, tg * 64:(tg + 1) * 64].unsqueeze(1).broadcast_to([128, 4, 64])
                    sa_ = sinbuf[:, tg * 64:(tg + 1) * 64].unsqueeze(1).broadcast_to([128, 4, 64])
                    sch.op("vector", lambda v, r13=r13, srb3=srb3, ca_=ca_: v.tensor_mul(
                        r13, srb3[:, :, 0:64], ca_), reads=[srbk, "cosbuf"],
                        writes=W(ar_ms, "rtmp1", 14848, 15872))
                    sch.op("vector", lambda v, r23=r23, srb3=srb3, sa_=sa_: v.tensor_mul(
                        r23, srb3[:, :, 64:128], sa_), reads=[srbk, "sinbuf"],
                        writes=W(ar_ms, "rtmp2", 15872, 16896))
                    sch.op("vector", lambda v, dst3=dst3, r13=r13, r23=r23: v.tensor_sub(
                        dst3[:, :, 0:64], r13, r23), reads=["rtmp1", "rtmp2"],
                        writes=W(ar_x1, dkey, dlo, dlo + 2048))
                    sch.op("vector", lambda v, r13=r13, srb3=srb3, ca_=ca_: v.tensor_mul(
                        r13, srb3[:, :, 64:128], ca_), reads=[srbk, "cosbuf"],
                        writes=W(ar_ms, "rtmp1", 14848, 15872))
                    sch.op("vector", lambda v, r23=r23, srb3=srb3, sa_=sa_: v.tensor_mul(
                        r23, srb3[:, :, 0:64], sa_), reads=[srbk, "sinbuf"],
                        writes=W(ar_ms, "rtmp2", 15872, 16896))
                    sch.op("vector", lambda v, dst3=dst3, r13=r13, r23=r23: v.tensor_add(
                        dst3[:, :, 64:128], r13, r23), reads=["rtmp1", "rtmp2"],
                        writes=[dkey])

        def emit_mid_tile(sh, t):
            par = t % 2
            qrk, krk, vbk = f"qr{sh}_{t}", f"kr{sh}_{t}", f"vb{sh}_{t}"
            nqk, nkk = f"n2q{par}", f"n2k{par}"
            iqk, ikk = f"invq{par}", f"invk{par}"
            n2q_ = n2q[:, par * 8:par * 8 + 8]
            n2k_ = n2k[:, par * 8:par * 8 + 8]
            invq_ = invq[:, par * 8:par * 8 + 8]
            invk_ = invk[:, par * 8:par * 8 + 8]
            sch.op("vector", lambda v, t=t: v.tensor_mul(sqq, qr_t(t), qr_t(t)),
                   reads=[qrk], writes=W(ar_ms, "sqq", 20992, 23040))
            sch.op("vector", lambda v, n2q_=n2q_: v.reduce_sum(
                n2q_, sqq.rearrange("p (h d) -> p h d", h=8), axis=AX.X),
                reads=["sqq"], writes=[nqk])
            sch.op("vector", lambda v, t=t: v.tensor_mul(sqk, kr_t(t), kr_t(t)),
                   reads=[krk], writes=W(ar_ms, "sqk", 23040, 25088))
            sch.op("vector", lambda v, n2k_=n2k_: v.reduce_sum(
                n2k_, sqk.rearrange("p (h d) -> p h d", h=8), axis=AX.X),
                reads=["sqk"], writes=[nkk])
            for (n2_, inv_, nk, ik) in ((n2q_, invq_, nqk, iqk), (n2k_, invk_, nkk, ikk)):
                sch.op("vector", lambda v, n2_=n2_: v.tensor_scalar_max(n2_, n2_, 1e-12),
                       reads=[nk], writes=[nk])
                sch.op("scalar", lambda s, n2_=n2_, inv_=inv_: s.activation(
                    inv_, n2_, AF.Sqrt), reads=[nk], writes=[ik + "s"])
                sch.op("vector", lambda v, inv_=inv_: v.reciprocal(inv_, inv_),
                       reads=[ik + "s"], writes=[ik])
            for h in range(8):
                sch.op("vector", lambda v, t=t, h=h, invq_=invq_: v.tensor_scalar(
                    qr_t(t)[:, h * 128:(h + 1) * 128], qr_t(t)[:, h * 128:(h + 1) * 128],
                    invq_[:, h:h + 1], None, ALU.mult), reads=[qrk, iqk], writes=[qrk])
                sch.op("vector", lambda v, t=t, h=h, invk_=invk_: v.tensor_scalar(
                    kr_t(t)[:, h * 128:(h + 1) * 128], kr_t(t)[:, h * 128:(h + 1) * 128],
                    invk_[:, h:h + 1], None, ALU.mult), reads=[krk, ikk], writes=[krk])
            trb = pb0 if par == 0 else pb1
            trn = "pb0" if par == 0 else "pb1"
            tra = ar_p0 if par == 0 else ar_p1
            trbb = trb[:].bitcast(BF16)
            for (srcf, dstb, sk, dk, dlo, bank) in (
                    (qr_t(t), qnT, qrk, "qnT", 0, 0), (kr_t(t), knT, krk, "knT", 2048, 1)):
                pk = f"ptr{trn}{bank}"
                def trans8(tt, srcf=srcf, bank=bank, trbb=trbb):
                    for h in range(8):
                        ins = tt.transpose(trbb[:, bank * 1024 + h * 128:
                                                bank * 1024 + (h + 1) * 128],
                                           srcf[:, h * 128:(h + 1) * 128], identb[:])
                    return ins
                sch.op("tensor", trans8, reads=[sk, "identb"],
                       writes=W(tra, pk, bank * 2048, (bank + 1) * 2048))
                sch.op("vector", lambda v, dstb=dstb, bank=bank, trbb=trbb: v.tensor_copy(
                    dstb[:], trbb[:, bank * 1024:(bank + 1) * 1024]), reads=[pk],
                    writes=W(ar_ms, dk, dlo, dlo + 2048))
            for (srcT, sk, pcq, pk2) in ((qnT, "qnT", 1024, "prq" + trn),
                                         (knT, "knT", 1536, "prk" + trn)):
                def rmm(tt, srcT=srcT, pcq=pcq, trb=trb):
                    for h in range(8):
                        ins = tt.matmul(
                            trb[64 * (h % 2):64 * (h % 2) + 64,
                                pcq + (h // 2) * 128: pcq + (h // 2) * 128 + 128],
                            rtc[:], srcT[:, h * 128:(h + 1) * 128],
                            start=True, stop=True)
                    return ins
                sch.op("tensor", rmm, reads=[sk, "rtc"],
                       writes=W(tra, pk2, pcq * 4, pcq * 4 + 2048))
            for (pcq, dstb, dk, dlo, upk) in (
                    (1024, qpt, "qpt", 4096, "prq" + trn),
                    (1536, kpt, "kpt", 6144, "prk" + trn)):
                up = trb[:, pcq:pcq + 512]
                sch.op("vector", lambda v, up=up: v.tensor_copy(ti32[:], up),
                       reads=[upk], writes=W(ar_ms, "ti32", 10752, 12800))
                sch.op("vector", lambda v: v.tensor_copy(kfb[:], ti32[:]),
                       reads=["ti32"], writes=W(ar_ms, "kfb", 12800, 14848))
                sch.op("vector", lambda v, up=up: v.tensor_sub(wfr[:], up, kfb[:]),
                       reads=[upk, "kfb"], writes=W(ar_ms, "wfr", 14848, 16896))
                sch.op("scalar", lambda s: s.activation(s1b[:], wfr[:], AF.Sin,
                                                        scale=cpi[:, 0:1]),
                       reads=["wfr", "cpi"], writes=W(ar_ms, "s1b", 16896, 17920))
                sch.op("scalar", lambda s: s.activation(wabs[:], wfr[:], AF.Abs),
                       reads=["wfr"], writes=W(ar_ms, "wabs", 10752, 12800))
                sch.op("scalar", lambda s: s.activation(
                    c1b[:], wabs[:], AF.Sin, scale=cpi[:, 1:2], bias=cpi[:, 2:3]),
                    reads=["wabs", "cpi"], writes=W(ar_ms, "c1b", 17920, 18944))
                for half in range(2):
                    s1h = s1b[64 * half:64 * half + 64, :].rearrange("p (b d) -> p b d", b=4)
                    c1h = c1b[64 * half:64 * half + 64, :].rearrange("p (b d) -> p b d", b=4)
                    sq1h = sq1b[64 * half:64 * half + 64, :].rearrange("p (b d) -> p b d", b=4)
                    dsin = dstb[0:64, :].rearrange("p (h d) -> p h d", h=8)[:, half::2, :]
                    dcos = dstb[64:128, :].rearrange("p (h d) -> p h d", h=8)[:, half::2, :]
                    sch.op("vector", lambda v, s1h=s1h, c1h=c1h, dsin=dsin: v.tensor_mul(
                        dsin, s1h, c1h), reads=["s1b", "c1b"],
                        writes=W(ar_ms, dk, dlo, dlo + 2048))
                    sch.op("vector", lambda v, s1h=s1h, sq1h=sq1h: v.tensor_mul(
                        sq1h, s1h, s1h), reads=["s1b"],
                        writes=W(ar_ms, "sq1b", 18944, 19968))
                    sch.op("vector", lambda v, sq1h=sq1h, dcos=dcos: v.tensor_scalar(
                        dcos, sq1h, -1.0, 0.5, ALU.mult, ALU.add),
                        reads=["sq1b"], writes=[dk])
            for h in range(8):
                sch.op("vector", lambda v, h=h: v.tensor_copy(
                    qparr[:].rearrange("p (g c) -> p g c", g=8)[:, :, h * 16:(h + 1) * 16],
                    qpt[:, h * 128:(h + 1) * 128].rearrange("p (g j) -> p g j", g=8)),
                    reads=["qpt"], writes=W(ar_ms, "qparr", 25088, 27136))
                sch.op("vector", lambda v, h=h: v.tensor_copy(
                    kparr[:].rearrange("p (g j e) -> p g j e", g=8, j=16)[:, :, :, h:h + 1],
                    kpt[:, h * 128:(h + 1) * 128].rearrange("p (g j) -> p g j", g=8).unsqueeze(3)),
                    reads=["kpt"], writes=W(ar_ms, "kparr", 27136, 29184))
            for g in range(8):
                sch.dma("gpsimd", lambda gg, t=t, g=g: gg.dma_start(
                    v16[:, g * 128:(g + 1) * 128], vb_t(t)[g * 16:(g + 1) * 16, :]),
                    reads=[vbk],
                    writes=W(ar_ms, f"v16_{g}", 8192 + g * 256, 8192 + (g + 1) * 256))
            for g in range(8):
                pa = trb[:, (g % 2) * 512:(g % 2) * 512 + 128]
                pak = f"pa{trn}{g % 2}"
                sch.op("tensor", lambda tt, g=g, pa=pa: tt.matmul(
                    pa, kparr[:, g * 128:(g + 1) * 128], qparr[:, g * 128:(g + 1) * 128],
                    start=True, stop=True), reads=["kparr", "qparr"],
                    writes=W(tra, pak, (g % 2) * 2048, (g % 2) * 2048 + 2048))
                aslot = a16[g % 2]
                sch.op("vector", lambda v, pa=pa, aslot=aslot: v.tensor_mul(
                    aslot, pa, amaskb[:]), reads=[pak, "amaskb"],
                    writes=W(ar_ms, f"a16_{g % 2}", 10240 + (g % 2) * 256,
                             10240 + (g % 2 + 1) * 256))
                po = trb[:, 1024 + (g % 2) * 512:1024 + (g % 2) * 512 + 128]
                pok = f"po{trn}{g % 2}"
                sch.op("tensor", lambda tt, g=g, po=po, aslot=aslot: tt.matmul(
                    po, v16[:, g * 128:(g + 1) * 128], aslot, start=True, stop=True),
                    reads=[f"v16_{g}", f"a16_{g % 2}"],
                    writes=W(tra, pok, 4096 + (g % 2) * 2048, 4096 + (g % 2 + 1) * 2048))
                toff = sh * 1024 + t * 128 + g * 16
                sch.op("vector", lambda v, po=po, toff=toff: v.tensor_copy(
                    outT[:].rearrange("p (hl tt2) -> p hl tt2", hl=8)[:, :, toff:toff + 16],
                    po[:].rearrange("p (hl j) -> p hl j", hl=8)),
                    reads=[pok], writes=W(ar_ot, "outT", 0, 32768))

        for t in range(8):
            emit_ln1_tile(0, t, True, None, lnxsA,
                          lambda: W(ar_x1, "lnxs", 57344, 65536),
                          cacol, sacol, ["col1", "col0"])
        if stage == "ln1":
            sch.dma("sync", lambda s: s.dma_start(dbg[:, 0:8192], ahtf[:, 0:8192]),
                    reads=[f"hT{ic}_{t}" for ic in range(NIC) for t in range(8)])
            sch.wait_all_dmas("sync")
            sch.finalize(blk)
            return nc
        for ob in range(6):
            emit_qkv_block(0, ob)
        if stage == "qkv":
            sch.dma("sync", lambda s: s.dma_start(dbg[:, 0:12288], ax1f[:, 0:12288]),
                    reads=[f"qr0_{t}" for t in range(8)]
                    + [f"kr0_{t}" for t in range(8)]
                    + [f"vb0_{t}" for t in range(8)])
            sch.wait_all_dmas("sync")
            sch.finalize(blk)
            return nc
        for t in range(8):
            emit_mid_tile(0, t)
            emit_ln1_tile(1, t, True, None, lnxsA,
                          lambda: W(ar_x1, "lnxs", 57344, 65536),
                          cacol, sacol, ["col1", "col0"])
        for ob in range(6):
            emit_qkv_block(1, ob)
        for t in range(8):
            emit_mid_tile(1, t)
            emit_mod_jb(8 + 2 * t)
            emit_mod_jb(9 + 2 * t)

        if stage == "attn":
            sch.dma("sync", lambda s: s.dma_start(dbg[:, 0:8192], aotb[:].bitcast(F32)),
                    reads=["outT"])
            sch.wait_all_dmas("sync")
            sch.finalize(blk)
            return nc

        for hl in range(8):
            sch.op("vector", lambda v, hl=hl: v.tensor_copy(
                wo_buf[:, hl * 2048:(hl + 1) * 2048].rearrange("p (r q) -> p r q", r=16),
                outT[:, hl * 2048:(hl + 1) * 2048].rearrange("p (q r) -> p r q", q=128)),
                reads=["outT"], writes=W(ar_ms, f"wob{hl}", hl * 4096, (hl + 1) * 4096))
        for t in range(8):
            sch.dma("sync", lambda s, t=t: s.dma_start(
                x1t(t), dram["xown"][t * 128:(t + 1) * 128, :]),
                writes=W(ar_x1, f"x1_{t}", t * 8192, (t + 1) * 8192))

        sch.dma("gpsimd", lambda g: g.dma_start(gac, dmod[2:3, :].broadcast_to([128, D])),
                reads=["dmod2"], writes=W(ar_ht, "gac", 0, 8192))
        for ob in range(4):
            wkey = f"awblk{ob % 2}"
            sch.dma("gpsimd", lambda g, ob=ob: g.dma_start(
                blkbuf(ob % 2)[:].rearrange("p (c n) -> p c n", c=NIC), dram["wot"][ob]),
                writes=W(ar_aw, wkey, (ob % 2) * 16384, (ob % 2 + 1) * 16384))
            brow = brows[0:1, (ob % 2) * 512:(ob % 2) * 512 + 512]
            brkey = f"brow{ob % 2}"
            sch.dma("gpsimd", lambda g, ob=ob, brow=brow: g.dma_start(
                brow, dram["bo"][0:1, ob * 512:(ob + 1) * 512]), writes=[brkey])
            for t in range(8):
                pcol = (t % 2) * 512
                pv = pb0[:, pcol:pcol + 512]
                pk = f"pqkv{t % 2}"

                def womm(tt, t=t, ob=ob, pv=pv, brow=brow):
                    for r in range(16):
                        tt.matmul(pv, wo_buf[:, t * 2048 + r * 128: t * 2048 + (r + 1) * 128],
                                  blkbuf(ob % 2)[:, r * 512:(r + 1) * 512],
                                  start=(r == 0), stop=False)
                    return tt.matmul(pv, onesr[:], brow,
                                     start=False, stop=True)
                sch.op("tensor", womm, reads=[wkey, f"wob{t}", "onesr", brkey],
                       writes=W(ar_p0, pk, pcol * 4, pcol * 4 + 2048))
                ev = evt[t % 2]
                sch.op("vector", lambda v, pv=pv, ev=ev, ob=ob: v.tensor_mul(
                    ev, pv, gac[:, ob * 512:(ob + 1) * 512]),
                    reads=[pk, "gac"],
                    writes=W(ar_ht, f"evt{t % 2}", 8192 + (t % 2) * 2048,
                             8192 + (t % 2 + 1) * 2048))
                xsl = x1t(t)[:, ob * 512:(ob + 1) * 512]
                sch.op("vector", lambda v, xsl=xsl, ev=ev: v.tensor_add(xsl, xsl, ev),
                       reads=[f"evt{t % 2}", f"x1_{t}"], writes=[f"x1_{t}"])

        if stage == "wo":
            for t in range(8):
                sch.dma("sync", lambda s, t=t: s.dma_start(
                    dbg[:, t * 2048:(t + 1) * 2048], x1t(t)), reads=[f"x1_{t}"])
            sch.wait_all_dmas("sync")
            sch.finalize(blk)
            return nc

        for t in range(8):
            ln_tile(x1t(t), [f"x1_{t}"], lnxsB,
                    lambda: W(ar_ms, "lnxs", 16384, 24576))
            transpose_h_tile(t, lnxsB, cbcol, sbcol, ["col4", "col3"])

        sch.dma("gpsimd", lambda g: g.dma_start(awb[:, 0:128], dram["gwt"]),
                writes=W(ar_aw, "gwtb", 0, 256))
        for t in range(8):
            pg = pb1[:, 0:8]

            def gmm(tt, t=t, pg=pg):
                for ic in range(NIC):
                    ins = tt.matmul(pg, hT_tile(ic, t), awb[:, ic * 8:(ic + 1) * 8],
                                    start=(ic == 0), stop=(ic == NIC - 1))
                return ins
            sch.op("tensor", gmm, reads=["gwtb"] + [f"hT{ic}_{t}" for ic in range(NIC)],
                   writes=W(ar_p1, "pg", 0, 2048))
            gs = gsc[:, t * 8:(t + 1) * 8]
            sch.op("scalar", lambda s, pg=pg, gs=gs: s.activation(gs, pg, AF.Sigmoid),
                   reads=["pg"], writes=[f"gsc{t}"])
            bi = gtmp[:, 0:8]
            sch.op("vector", lambda v, gs=gs, bi=bi: v.tensor_add(bi, gs, ebc[:]),
                   reads=[f"gsc{t}", "ebc"], writes=["gbi"])
            sch.op("vector", lambda v, bi=bi: v.reduce_max(gm[:, 0:1], bi, axis=AX.X),
                   reads=["gbi"], writes=["gm0"])
            e1 = gtmp[:, 8:16]
            sch.op("vector", lambda v, bi=bi, e1=e1: v.tensor_scalar(
                e1, bi, gm[:, 0:1], None, ALU.is_ge), reads=["gbi", "gm0"], writes=["ge1"])
            msk = gtmp[:, 16:24]
            sch.op("vector", lambda v, e1=e1, bi=bi, msk=msk: v.scalar_tensor_tensor(
                msk, e1, 1e30, bi, ALU.mult, ALU.subtract),
                reads=["ge1", "gbi"], writes=["gmsk"])
            sch.op("vector", lambda v, msk=msk: v.tensor_reduce(
                gm[:, 1:2], msk, axis=AX.X, op=ALU.min), reads=["gmsk"], writes=["gm1"])
            e2 = gtmp[:, 24:32]
            sch.op("vector", lambda v, msk=msk, e2=e2: v.tensor_scalar(
                e2, msk, gm[:, 1:2], None, ALU.is_le), reads=["gmsk", "gm1"], writes=["ge2"])
            sch.op("vector", lambda v, e1=e1, e2=e2: v.tensor_add(gtmp[:, 0:8], e1, e2),
                   reads=["ge1", "ge2"], writes=["gmaskf"])
            sw_sl = sw_t[:, t * 8:(t + 1) * 8]
            sch.op("vector", lambda v, gs=gs, sw_sl=sw_sl: v.tensor_mul(
                sw_sl, gs, gtmp[:, 0:8]), reads=[f"gsc{t}", "gmaskf"], writes=[f"sw{t}"])
            pvt = pb1[0:8, 512:640]
            sch.op("tensor", lambda tt, sw_sl=sw_sl, pvt=pvt: tt.transpose(
                pvt, sw_sl, identf[:]), reads=[f"sw{t}", "identf"],
                writes=W(ar_p1, "pst", 2048, 4096))
            sch.op("vector", lambda v, t=t, pvt=pvt: v.tensor_copy(
                sT[:, t * 128:(t + 1) * 128], pvt), reads=["pst"],
                writes=W(ar_ms, "sT", 24576, 26624))
        sch.dma("gpsimd", lambda s: s.dma_start(sdram, sT[:]), reads=["sT"], writes=["sdram"])

        if stage == "gate":
            sch.dma("gpsimd", lambda g: g.dma_start(dbg[0:8, 0:1024], sdram), reads=["sdram"])
            sch.wait_all_dmas("sync")
            sch.finalize(blk)
            return nc

        sch.dma("gpsimd", lambda g: g.dma_start(gbc, dmod[5:6, :].broadcast_to([128, D])),
                reads=["dmod5"], writes=W(ar_ms, "gbc", 8192, 16384))
        pairs = [(0, 1), (2, 3), (4, 5), (6, 7), ("sh",)]
        h2keys = [f"hT{ic}_{t}" for ic in range(NIC) for t in range(8)]
        for pair in pairs:
            for ei, e in enumerate(pair):
                is_sh = (e == "sh")
                hebuf = heA if ei == 0 else heB
                hekey = f"he{ei}"
                sbkey = f"sbc{ei}"
                if not is_sh:
                    sch.dma("gpsimd", lambda g, e=e, ei=ei: g.dma_start(
                        sbc[ei], sdram[e:e + 1, :].broadcast_to([128, TOWN])),
                        reads=["sdram"],
                        writes=W(ar_ms, sbkey, ei * 4096, (ei + 1) * 4096))
                w1src = dram["sw1i"] if is_sh else dram["w1i"][e]
                w3src = dram["sw3i"] if is_sh else dram["w3i"][e]
                for hc in range(8):
                    qi = hc // 2
                    s1i = (qi % 2) * 2 + hc % 2
                    s3i = 4 + s1i
                    k1 = f"aws{(qi % 2) * 2}"
                    k3 = f"aws{4 + (qi % 2) * 2}"
                    if hc % 2 == 0:
                        s1lo = (qi % 2) * 2 * 4096
                        sch.dma("gpsimd", lambda g, w1src=w1src, hc=hc, s1lo=s1lo: g.dma_start(
                            awb[:, s1lo // 2:s1lo // 2 + 4096].rearrange(
                                "p (c q n) -> p c q n", c=2, q=NIC),
                            w1src[hc:hc + 2].rearrange("c p q n -> p c q n")),
                            writes=W(ar_aw, k1, s1lo, s1lo + 8192))
                        s3lo = 16384 + (qi % 2) * 2 * 4096
                        sch.dma("gpsimd", lambda g, w3src=w3src, hc=hc, s3lo=s3lo: g.dma_start(
                            awb[:, s3lo // 2:s3lo // 2 + 4096].rearrange(
                                "p (c q n) -> p c q n", c=2, q=NIC),
                            w3src[hc:hc + 2].rearrange("c p q n -> p c q n")),
                            writes=W(ar_aw, k3, s3lo, s3lo + 8192))
                    pu1 = pb0[:, (hc % 2) * 1024:(hc % 2) * 1024 + 1024]
                    pu3 = pb1[:, (hc % 2) * 1024:(hc % 2) * 1024 + 1024]
                    pu1k = f"pu1{hc % 2}"
                    pu3k = f"pu3{hc % 2}"

                    def umm(tt, sl, pu):
                        for ic in range(NIC):
                            for th in range(2):
                                ins = tt.matmul(
                                    pu[:, th * 512:(th + 1) * 512],
                                    slot(sl)[:, ic * 128:(ic + 1) * 128],
                                    ahtb[:, ic * 1024 + th * 512: ic * 1024 + (th + 1) * 512],
                                    start=(ic == 0), stop=(ic == NIC - 1))
                        return ins
                    sch.op("tensor", lambda tt, s1i=s1i, pu1=pu1: umm(tt, s1i, pu1),
                           reads=[k1] + h2keys,
                           writes=W(ar_p0, pu1k, (hc % 2) * 4096, (hc % 2 + 1) * 4096))
                    sch.op("tensor", lambda tt, s3i=s3i, pu3=pu3: umm(tt, s3i, pu3),
                           reads=[k3] + h2keys,
                           writes=W(ar_p1, pu3k, (hc % 2) * 4096, (hc % 2 + 1) * 4096))
                    hsl = hebuf[:, hc * 1024:(hc + 1) * 1024]
                    he_w = W(ar_ot, hekey, ei * 16384 + 0, ei * 16384 + 16384) if hc == 0 \
                        else [hekey]
                    if is_sh:
                        sch.op("scalar", lambda s, pu1=pu1: s.activation(scrB, pu1, AF.Silu),
                               reads=[pu1k], writes=W(ar_ms, "scrB", 20480, 24576))
                        sch.op("vector", lambda v, hsl=hsl, pu3=pu3: v.tensor_mul(
                            hsl, scrB, pu3), reads=["scrB", pu3k], writes=he_w)
                    else:
                        sch.op("vector", lambda v, pu1=pu1, ei=ei: v.tensor_mul(
                            scrA, pu1, sbc[ei]), reads=[pu1k, sbkey],
                            writes=W(ar_ms, "scrA", 16384, 20480))
                        sch.op("scalar", lambda s: s.activation(scrB, scrA, AF.Silu),
                               reads=["scrA"], writes=W(ar_ms, "scrB", 20480, 24576))
                        sch.op("vector", lambda v, pu3=pu3, ei=ei: v.tensor_mul(
                            scrA, pu3, sbc[ei]), reads=[pu3k, sbkey], writes=["scrA"])
                        sch.op("vector", lambda v, hsl=hsl: v.tensor_mul(hsl, scrB, scrA),
                               reads=["scrA", "scrB"], writes=he_w)
            w2s = [dram["sw2i"] if e == "sh" else dram["w2i"][e] for e in pair]
            w2ids = [8 if e == "sh" else e for e in pair]
            for t2 in range(4):
                taus = (2 * t2, 2 * t2 + 1)
                for ei, e in enumerate(pair):
                    hebuf = heA if ei == 0 else heB
                    hekey = f"he{ei}"
                    for hc in range(8):
                        hf = hc // 4
                        base = (ei * 2 + hf) % 2
                        ws = base * 4 + hc % 4
                        wkey = f"aws{base * 4}"
                        w2id = w2ids[ei]
                        if hc % 4 == 0:
                            lo = base * 4 * 4096
                            if t2 == 0:
                                sch.dma("gpsimd", lambda g, w2=w2s[ei], hc=hc, lo=lo:
                                        g.dma_start(
                                            awb[:, lo // 2:lo // 2 + 8192].rearrange(
                                                "p (c n) -> p c n", c=4),
                                            w2[hc:hc + 4].rearrange("c p n -> p c n")),
                                        writes=W(ar_aw, wkey, lo, lo + 16384))
                                sch.dma("scalar", lambda s, w2id=w2id, hc=hc, lo=lo:
                                        s.dma_start(
                                            w2scr[w2id, hc:hc + 4].rearrange("c p n -> p c n"),
                                            awb[:, lo // 2:lo // 2 + 8192].rearrange(
                                                "p (c n) -> p c n", c=4)),
                                        reads=[wkey], writes=[f"w2s{w2id}_{hf}"])
                            else:
                                sch.dma("sync", lambda s, w2id=w2id, hc=hc, lo=lo:
                                        s.dma_start(
                                            awb[:, lo // 2:lo // 2 + 8192].rearrange(
                                                "p (c n) -> p c n", c=4),
                                            w2scr[w2id, hc:hc + 4].rearrange("c p n -> p c n")),
                                        reads=[f"w2s{w2id}_{hf}"],
                                        writes=W(ar_aw, wkey, lo, lo + 16384))
                        start_f = (ei == 0 and hc == 0)
                        stop_f = (ei == len(pair) - 1 and hc == 7)
                        for tau in taus:
                            pw = pb0 if tau % 2 == 0 else pb1
                            pwk = f"pw{tau % 2}"
                            arp = ar_p0 if tau % 2 == 0 else ar_p1

                            def w2mm(tt, hebuf=hebuf, hc=hc, tau=tau, ws=ws, pw=pw,
                                     start_f=start_f, stop_f=stop_f):
                                for ob in range(4):
                                    ins = tt.matmul(
                                        pw[:, ob * 512:(ob + 1) * 512],
                                        hebuf[:, hc * 1024 + tau * 128:
                                              hc * 1024 + tau * 128 + 128],
                                        slot(ws)[:, ob * 512:(ob + 1) * 512],
                                        start=start_f, stop=stop_f)
                                return ins
                            sch.op("tensor", w2mm, reads=[wkey, hekey],
                                   writes=W(arp, pwk, 0, 8192))
                for tau in taus:
                    pw = pb0 if tau % 2 == 0 else pb1
                    pwk = f"pw{tau % 2}"
                    sch.op("vector", lambda v, pw=pw: v.tensor_mul(scr2, pw[:], gbc),
                           reads=[pwk, "gbc"], writes=W(ar_ms, "scr2", 24576, 32768))
                    sch.op("vector", lambda v, tau=tau: v.tensor_add(
                        x1t(tau), x1t(tau), scr2), reads=["scr2", f"x1_{tau}"],
                        writes=[f"x1_{tau}"])

        for t in range(8):
            sch.dma("sync", lambda s, t=t: s.dma_start(
                out[t * 128:(t + 1) * 128, :], x1t(t)), reads=[f"x1_{t}"])
        sch.wait_all_dmas("sync")
        sch.finalize(blk)
    return nc


def kernel(**inputs):
    in_maps = _prep(inputs)
    nc = _build(STAGE)
    res = run_bass_kernel_spmd(nc, in_maps, core_ids=list(range(8)))
    outb = np.empty((B, S, D), np.float32)
    for core in range(8):
        b, H = core // 2, core % 2
        outb[b, H * 1024:(H + 1) * 1024, :] = res.results[core]["out"]
    return outb


# version 05
# speedup vs baseline: 1.0350x; 1.0350x over previous
import contextlib
import os
import numpy as np
import concourse.bass as bass
import concourse.mybir as mybir
from concourse.bass_utils import run_bass_kernel_spmd

F32 = mybir.dt.float32
BF16 = mybir.dt.bfloat16
I32 = mybir.dt.int32
AF = mybir.ActivationFunctionType
ALU = mybir.AluOpType
AX = mybir.AxisListType

PI = float(np.pi)
INV2PI = float(1.0 / (2.0 * np.pi))
D = 2048
S = 2048
B = 4
TOWN = 1024
NIC = 16
E = 8
EPS = 1e-5
STAGE = os.environ.get("KSTAGE", "full")

NLANES = {"sync": 8, "gpsimd": 8, "scalar": 4}


class _Stream:
    __slots__ = ("name", "sem_c", "c", "lanes", "lane_tot", "lane_rr", "items", "known")

    def __init__(self, name, sem_c, lanes):
        self.name = name
        self.sem_c = sem_c
        self.c = 0
        self.lanes = lanes
        self.lane_tot = [0] * len(lanes)
        self.lane_rr = 0
        self.items = []
        self.known = {}


class Sched:
    ENGINES = ("tensor", "vector", "scalar", "gpsimd", "sync")

    def __init__(self, nc, stack):
        self.nc = nc
        self.streams = {}
        for name in self.ENGINES:
            sem_c = None if name == "sync" else stack.enter_context(
                nc.semaphore(f"semc_{name}"))
            lanes = [stack.enter_context(nc.semaphore(f"semd_{name}{i}"))
                     for i in range(NLANES.get(name, 0))]
            self.streams[name] = _Stream(name, sem_c, lanes)
        self.last_writer = {}
        self.readers = {}
        self.n_ops = 0
        self.n_waits = 0

    def _emit_waits(self, me, deps):
        for sid in sorted(deps):
            sem, val = deps[sid]
            if me.known.get(sid, 0) >= val:
                continue
            me.items.append(("wait", sem, val))
            me.known[sid] = val
            self.n_waits += 1

    def _gather_deps(self, reads, writes):
        deps = {}

        def add(sem, val):
            sid = id(sem)
            if sid not in deps or deps[sid][1] < val:
                deps[sid] = (sem, val)

        for k in reads:
            lw = self.last_writer.get(k)
            if lw is not None:
                add(lw[1], lw[2])
        for k in writes:
            lw = self.last_writer.get(k)
            if lw is not None:
                add(lw[1], lw[2])
            for (sem, val) in self.readers.get(k, {}).values():
                add(sem, val)
        return deps

    def _record(self, reads, writes, dep):
        for k in writes:
            self.last_writer[k] = dep
            self.readers[k] = {}
        for k in reads:
            self.readers.setdefault(k, {})[(dep[0], id(dep[1]))] = (dep[1], dep[2])
        self.n_ops += 1

    def op(self, stream, fn, reads=(), writes=()):
        me = self.streams[stream]
        self._emit_waits(me, self._gather_deps(reads, writes))
        me.c += 1
        me.items.append(("op", fn, me.sem_c, 1))
        self._record(reads, writes, (stream, me.sem_c, me.c))

    def dma(self, stream, fn, reads=(), writes=()):
        me = self.streams[stream]
        lane = me.lane_rr
        me.lane_rr = (me.lane_rr + 1) % len(me.lanes)
        deps = self._gather_deps(reads, writes)
        sem = me.lanes[lane]
        prev = me.lane_tot[lane]
        if prev and me.known.get(id(sem), 0) < prev:
            if id(sem) not in deps or deps[id(sem)][1] < prev:
                deps[id(sem)] = (sem, prev)
        self._emit_waits(me, deps)
        me.lane_tot[lane] += 16
        me.items.append(("op", fn, sem, 16))
        self._record(reads, writes, (stream, sem, me.lane_tot[lane]))

    def wait_all_dmas(self, stream):
        me = self.streams[stream]
        deps = {}
        for st in self.streams.values():
            for i, tot in enumerate(st.lane_tot):
                if tot:
                    deps[id(st.lanes[i])] = (st.lanes[i], tot)
        self._emit_waits(me, deps)

    def finalize(self, blk):
        for name in self.ENGINES:
            st = self.streams[name]
            if not st.items:
                continue

            def body(eng, st=st):
                for item in st.items:
                    if item[0] == "wait":
                        eng.wait_ge(item[1], item[2])
                    else:
                        _, fn, sem, inc = item
                        ins = fn(eng)
                        assert ins is not None, f"op on {st.name} returned None"
                        ins.then_inc(sem, inc)
            getattr(blk, name)(body)


class Arena:

    def __init__(self, gran=1):
        self.live = {}
        self.gran = gran

    def claim(self, key, lo, hi):
        g = self.gran
        lo = (lo // g) * g
        hi = ((hi + g - 1) // g) * g
        extra = []
        for k, (a, b) in list(self.live.items()):
            if k != key and a < hi and lo < b:
                extra.append(k)
                if lo <= a and b <= hi:
                    del self.live[k]
        cur = self.live.get(key)
        if cur:
            lo, hi = min(lo, cur[0]), max(hi, cur[1])
        self.live[key] = (lo, hi)
        return extra



def _wimg(WT, obs):
    K, O = WT.shape
    return np.ascontiguousarray(
        WT.reshape(K // 128, 128, O // obs, obs).transpose(2, 1, 0, 3))


def _prep(inputs):
    f32 = lambda a: np.ascontiguousarray(np.asarray(a, dtype=np.float32))
    x = f32(inputs["L0_attn_hidden_states"])
    Wq = f32(inputs["L0_attn_Wq"])
    bq = f32(inputs["L0_attn_bq"])
    Wk = f32(inputs["L0_attn_Wk"])
    bk = f32(inputs["L0_attn_bk"])
    Wv = f32(inputs["L0_attn_Wv"])
    bv = f32(inputs["L0_attn_bv"])
    Wo = f32(inputs["L0_attn_Wo"])
    bo = f32(inputs["L0_attn_bo"])
    R = f32(inputs["L0_attn_rand_matrix"])
    ebias = f32(inputs["L0_ffn_expert_bias"])
    gw = f32(inputs["L0_ffn_gate_w"])
    w1 = f32(inputs["L0_ffn_w1"])
    w2 = f32(inputs["L0_ffn_w2"])
    w3 = f32(inputs["L0_ffn_w3"])
    sw1 = f32(inputs["L0_ffn_sw1"])
    sw2 = f32(inputs["L0_ffn_sw2"])
    sw3 = f32(inputs["L0_ffn_sw3"])
    c = f32(inputs["L0_c"])
    wmod = f32(inputs["L0_wmod"])

    shared = {}
    shared["wkt"] = _wimg(Wk.T.copy(), 512)
    shared["wvt"] = _wimg(Wv.T.copy(), 512)
    shared["wot"] = _wimg(Wo.T.copy(), 512)
    shared["wmod"] = _wimg(wmod, 512)
    w1i = np.empty((E, 8, 128, NIC, 128), np.float32)
    w3i = np.empty((E, 8, 128, NIC, 128), np.float32)
    w2i = np.empty((E, 8, 128, D), np.float32)
    for e in range(E):
        w1i[e] = _wimg(w1[e].T.copy(), 128)
        w3i[e] = _wimg(w3[e].T.copy(), 128)
        w2i[e] = w2[e].T.reshape(8, 128, D)
    shared["w1i"] = w1i
    shared["w3i"] = w3i
    shared["w2i"] = w2i
    shared["sw1i"] = _wimg(sw1.T.copy(), 128)
    shared["sw3i"] = _wimg(sw3.T.copy(), 128)
    shared["sw2i"] = np.ascontiguousarray(sw2.T.reshape(8, 128, D))
    shared["gwt"] = np.ascontiguousarray(
        gw.T.reshape(NIC, 128, 8).transpose(1, 0, 2)).reshape(128, 128)
    shared["ebias"] = ebias.reshape(1, 8)
    shared["rt"] = R.T.copy()
    shared["bk"] = bk.reshape(1, 1024)
    shared["bv"] = bv.reshape(1, 1024)
    shared["bo"] = bo.reshape(1, 2048)
    inv_freq = 1.0 / (10000.0 ** (np.arange(0, 128, 2, dtype=np.float64) / 128))
    pos = np.arange(S, dtype=np.float64)[:, None] * inv_freq[None, :]
    emb = np.concatenate([pos, pos], axis=1)
    shared["cost"] = np.ascontiguousarray(
        np.cos(pos).astype(np.float32).reshape(16, 128, 64).transpose(1, 0, 2))
    shared["sint"] = np.ascontiguousarray(
        np.sin(pos).astype(np.float32).reshape(16, 128, 64).transpose(1, 0, 2))
    shared["ident"] = np.eye(128, dtype=np.float32)
    am = np.zeros((128, 128), np.float32)
    for j in range(16):
        for hk in range(8):
            am[j * 8 + hk].reshape(8, 16)[:, j] = 8.0 / 128.0
    shared["amask"] = am

    in_maps = []
    for core in range(8):
        b, H = core // 2, core % 2
        m = dict(shared)
        m["xin"] = x[b]
        m["xown"] = np.ascontiguousarray(x[b][H * 1024:(H + 1) * 1024])
        m["crowp"] = np.ascontiguousarray(c[b].reshape(NIC, 128).T)
        m["wqt"] = _wimg(Wq.T[:, H * 1024:(H + 1) * 1024].copy(), 512)
        m["bq"] = np.ascontiguousarray(bq[H * 1024:(H + 1) * 1024].reshape(1, 1024))
        in_maps.append(m)
    return in_maps



def _build(stage):
    nc = bass.Bass("TRN2", target_bir_lowering=False, debug=False)
    dram = {}
    for name, shape in [
        ("xin", [S, D]), ("xown", [TOWN, D]), ("crowp", [128, 16]),
        ("wqt", [2, 128, NIC, 512]), ("bq", [1, 1024]),
        ("wkt", [2, 128, NIC, 512]), ("bk", [1, 1024]),
        ("wvt", [2, 128, NIC, 512]), ("bv", [1, 1024]),
        ("wot", [4, 128, NIC, 512]), ("bo", [1, 2048]),
        ("wmod", [24, 128, NIC, 512]), ("rt", [128, 64]),
        ("w1i", [E, 8, 128, NIC, 128]), ("w3i", [E, 8, 128, NIC, 128]),
        ("w2i", [E, 8, 128, D]),
        ("sw1i", [8, 128, NIC, 128]), ("sw3i", [8, 128, NIC, 128]),
        ("sw2i", [8, 128, D]),
        ("gwt", [128, 128]), ("ebias", [1, 8]),
        ("cost", [128, 16, 64]), ("sint", [128, 16, 64]),
        ("ident", [128, 128]), ("amask", [128, 128]),
    ]:
        dram[name] = nc.dram_tensor(name, shape, F32, kind="ExternalInput").ap()
    out = nc.dram_tensor("out", [TOWN, D], F32, kind="ExternalOutput").ap()
    dmod = nc.dram_tensor("dmod", [6, D], F32, kind="Internal").ap()
    sdram = nc.dram_tensor("sdram", [E, TOWN], BF16, kind="Internal").ap()
    w2scr = nc.dram_tensor("w2scr", [9, 8, 128, D], BF16, kind="Internal").ap()
    dbg = None
    if stage != "full":
        dbg = nc.dram_tensor("dbg", [128, 16384], F32, kind="ExternalOutput").ap()

    with contextlib.ExitStack() as st:
        AX1 = st.enter_context(nc.sbuf_tensor("AX1", [128, 16384], F32))
        AHT = st.enter_context(nc.sbuf_tensor("AHT", [128, 16384], BF16))
        AOT = st.enter_context(nc.sbuf_tensor("AOT", [128, 16384], BF16))
        AMS = st.enter_context(nc.sbuf_tensor("AMS", [128, 16384], BF16))
        AW = st.enter_context(nc.sbuf_tensor("AW", [128, 16384], BF16))
        brows = st.enter_context(nc.sbuf_tensor("brows", [1, 1024], BF16))
        identb = st.enter_context(nc.sbuf_tensor("identb", [128, 128], BF16))
        identf = st.enter_context(nc.sbuf_tensor("identf", [128, 128], F32))
        rtc = st.enter_context(nc.sbuf_tensor("rtc", [128, 64], BF16))
        rtf = st.enter_context(nc.sbuf_tensor("rtf", [128, 64], F32))
        amaskb = st.enter_context(nc.sbuf_tensor("amaskb", [128, 128], F32))
        onesr = st.enter_context(nc.sbuf_tensor("onesr", [1, 128], BF16))
        scolf = st.enter_context(nc.sbuf_tensor("scolf", [128, 16], F32))
        scol = st.enter_context(nc.sbuf_tensor("scol", [128, 16], BF16))
        cacol = st.enter_context(nc.sbuf_tensor("cacol", [128, 16], F32))
        sacol = st.enter_context(nc.sbuf_tensor("sacol", [128, 16], F32))
        cbcol = st.enter_context(nc.sbuf_tensor("cbcol", [128, 16], F32))
        sbcol = st.enter_context(nc.sbuf_tensor("sbcol", [128, 16], F32))
        cosbuf = st.enter_context(nc.sbuf_tensor("cosbuf", [128, 1024], BF16))
        sinbuf = st.enter_context(nc.sbuf_tensor("sinbuf", [128, 1024], BF16))
        modr = st.enter_context(nc.sbuf_tensor("modr", [1, 1024], F32))
        n2q = st.enter_context(nc.sbuf_tensor("n2q", [128, 16], F32))
        n2k = st.enter_context(nc.sbuf_tensor("n2k", [128, 16], F32))
        invq = st.enter_context(nc.sbuf_tensor("invq", [128, 16], F32))
        invk = st.enter_context(nc.sbuf_tensor("invk", [128, 16], F32))
        lns = st.enter_context(nc.sbuf_tensor("lns", [128, 8], F32))
        gsc = st.enter_context(nc.sbuf_tensor("gsc", [128, 64], F32))
        gtmp = st.enter_context(nc.sbuf_tensor("gtmp", [128, 32], F32))
        gm = st.enter_context(nc.sbuf_tensor("gm", [128, 2], F32))
        sw_t = st.enter_context(nc.sbuf_tensor("sw_t", [128, 64], F32))
        ebc = st.enter_context(nc.sbuf_tensor("ebc", [128, 8], F32))
        cpi = st.enter_context(nc.sbuf_tensor("cpi", [128, 3], F32))
        pb0 = st.enter_context(nc.psum_tensor("pb0", [128, 2048], F32))
        pb1 = st.enter_context(nc.psum_tensor("pb1", [128, 2048], F32))
        blk = st.enter_context(nc.Block())
        sch = Sched(nc, st)

        ar_x1 = Arena()
        ar_ht = Arena()
        ar_ot = Arena()
        ar_ms = Arena()
        ar_aw = Arena()
        ar_p0 = Arena(gran=2048)
        ar_p1 = Arena(gran=2048)

        ax1f = AX1[:]
        ax1b = AX1[:].bitcast(BF16)
        ahtb = AHT[:]
        ahtf = AHT[:].bitcast(F32)
        aotb = AOT[:]
        amsb = AMS[:]
        amsf = AMS[:].bitcast(F32)
        amsi = AMS[:].bitcast(I32)
        awb = AW[:]
        pb0b = pb0[:].bitcast(BF16)
        pb1b = pb1[:].bitcast(BF16)

        def W(arena, key, lo, hi, extra=()):
            return [key] + arena.claim(key, lo, hi) + list(extra)

        qnT = amsb[:, 0:1024]
        knT = amsb[:, 1024:2048]
        qpt = amsb[:, 2048:3072]
        kpt = amsb[:, 3072:4096]
        v16 = amsb[:, 4096:5120]
        a16 = [amsb[:, 5120:5248], amsb[:, 5248:5376]]
        ti32 = amsi[:, 2688:3200]
        wabs = amsf[:, 2688:3200]
        kfb = amsf[:, 3200:3712]
        wfr = amsf[:, 3712:4224]
        rtmp1 = amsf[:, 3712:3968]
        rtmp2 = amsf[:, 3968:4224]
        s1b = amsb[:, 8448:8960]
        c1b = amsb[:, 8960:9472]
        sq1b = amsb[:, 9472:9984]
        srb = amsb[:, 9984:10496]
        sqq = amsb[:, 10496:11520]
        sqk = amsb[:, 11520:12544]
        qparr = amsb[:, 12544:13568]
        kparr = amsb[:, 13568:14592]
        lnxsB = amsf[:, 4096:6144]
        wo_buf = amsb
        sbc = [amsf[:, 0:1024], amsf[:, 1024:2048]]
        gbc = amsf[:, 2048:4096]
        scrA = amsf[:, 4096:5120]
        scrB = amsf[:, 5120:6144]
        scr2 = amsf[:, 6144:8192]
        sT = amsb[0:8, 12288:13312]
        gac = ahtf[:, 0:2048]
        evt = [ahtf[:, 2048:2560], ahtf[:, 2560:3072]]

        x1t = lambda t: ax1f[:, t * 2048:(t + 1) * 2048]
        qr_t = lambda t: ax1b[:, t * 1024:(t + 1) * 1024]
        kr_t = lambda t: ax1b[:, 8192 + t * 1024:8192 + (t + 1) * 1024]
        vb_t = lambda t: ax1b[:, 16384 + t * 1024:16384 + (t + 1) * 1024]
        xstage = ax1f[:, 12288:14336]
        lnxsA = ax1f[:, 14336:16384]
        hT_tile = lambda ic, t: ahtb[:, ic * 1024 + t * 128: ic * 1024 + (t + 1) * 128]
        outT = aotb
        heA, heB = aotb[:, 0:8192], aotb[:, 8192:16384]
        slot = lambda i: awb[:, i * 2048:(i + 1) * 2048]
        blkbuf = lambda i: awb[:, i * 8192:(i + 1) * 8192]


        sch.dma("gpsimd", lambda g: g.dma_start(identb[:], dram["ident"]), writes=["identb"])
        sch.dma("sync", lambda s: s.dma_start(identf[:], dram["ident"]), writes=["identf"])
        sch.dma("sync", lambda s: s.dma_start(rtf[:], dram["rt"]), writes=["rtf"])
        sch.op("vector", lambda v: v.tensor_scalar_mul(rtc[:], rtf[:], INV2PI),
               reads=["rtf"], writes=["rtc"])
        sch.dma("sync", lambda s: s.dma_start(amaskb[:], dram["amask"]), writes=["amaskb"])
        sch.op("vector", lambda v: v.memset(onesr[:], 1.0), writes=["onesr"])
        sch.op("vector", lambda v: v.memset(cpi[:, 0:1], PI), writes=["cpi"])
        sch.op("vector", lambda v: v.memset(cpi[:, 1:2], -PI), writes=["cpi"])
        sch.op("vector", lambda v: v.memset(cpi[:, 2:3], PI / 2), writes=["cpi"])
        sch.dma("sync", lambda s: s.dma_start(scolf[:], dram["crowp"]), writes=["scolf"])
        sch.op("scalar", lambda s: s.activation(scol[:], scolf[:], AF.Silu),
               reads=["scolf"], writes=["scol"])

        sch.dma("gpsimd", lambda g: g.dma_start(ebc[:], dram["ebias"].broadcast_to([128, 8])),
                writes=["ebc"])
        sch.dma("gpsimd", lambda g: g.dma_start(
            cosbuf[:].rearrange("p (a b) -> p a b", a=16), dram["cost"]), writes=["cosbuf"])
        sch.dma("gpsimd", lambda g: g.dma_start(
            sinbuf[:].rearrange("p (a b) -> p a b", a=16), dram["sint"]), writes=["sinbuf"])

        pmod = pb1[0:1, 1536:2048]
        cols4 = {0: sacol, 1: cacol, 3: sbcol, 4: cbcol}

        def emit_mod_jb(jb):
            ws = jb % 2
            wkey = f"awblk{ws}"
            sch.dma("gpsimd", lambda g, jb=jb, ws=ws: g.dma_start(
                blkbuf(ws)[:].rearrange("p (c n) -> p c n", c=NIC), dram["wmod"][jb]),
                writes=W(ar_aw, wkey, ws * 16384, (ws + 1) * 16384))

            def modmm(t, ws=ws):
                for kc in range(NIC):
                    ins = t.matmul(pmod, scol[:, kc:kc + 1],
                                   blkbuf(ws)[:, kc * 512:(kc + 1) * 512],
                                   start=(kc == 0), stop=(kc == NIC - 1))
                return ins
            sch.op("tensor", modmm, reads=[wkey, "scol"],
                   writes=W(ar_p1, "pmod", 6144, 8192))
            v_idx, q_idx = jb // 4, jb % 4
            mslot = jb % 2
            mr = modr[0:1, mslot * 512:(mslot + 1) * 512]
            mkey = f"modr{mslot}"
            if v_idx in (1, 4):
                sch.op("vector", lambda v, mr=mr: v.tensor_scalar_add(mr, pmod, 1.0),
                       reads=["pmod"], writes=[mkey])
            elif v_idx in (2, 5):
                t1slot = 1 - mslot
                mt1 = modr[0:1, t1slot * 512:(t1slot + 1) * 512]
                mt1key = f"modr{t1slot}"
                sch.op("scalar", lambda s, mr=mr: s.activation(mr, pmod, AF.Sign),
                       reads=["pmod"], writes=[mkey])
                sch.op("scalar", lambda s, mt1=mt1: s.activation(mt1, pmod, AF.Abs),
                       reads=["pmod"], writes=[mt1key])
                sch.op("vector", lambda v, mt1=mt1: v.tensor_scalar_max(mt1, mt1, 0.25),
                       reads=[mt1key], writes=[mt1key])
                sch.op("vector", lambda v, mr=mr, mt1=mt1: v.tensor_mul(mr, mr, mt1),
                       reads=[mkey, mt1key], writes=[mkey])
            else:
                sch.op("vector", lambda v, mr=mr: v.tensor_copy(mr, pmod),
                       reads=["pmod"], writes=[mkey])
            sch.dma("scalar", lambda s, v_idx=v_idx, q_idx=q_idx, mr=mr: s.dma_start(
                dmod[v_idx:v_idx + 1, q_idx * 512:(q_idx + 1) * 512], mr),
                reads=[mkey], writes=[f"dmod{v_idx}"])
            if v_idx in cols4:
                colt = cols4[v_idx]
                ckey = f"col{v_idx}"
                for i in range(4):
                    pvt = pb1[:, 1024 + 64 * (i % 2):1024 + 64 * (i % 2) + 1]
                    sch.op("tensor", lambda tt, mr=mr, i=i, pvt=pvt: tt.transpose(
                        pvt, mr[0:1, i * 128:(i + 1) * 128], identf[0:1, 0:1]),
                        reads=[mkey, "identf"], writes=W(ar_p1, f"pcol{i % 2}", 4096, 6144))
                    sch.op("vector", lambda v, colt=colt, q_idx=q_idx, i=i, pvt=pvt:
                           v.tensor_copy(colt[:, q_idx * 4 + i:q_idx * 4 + i + 1], pvt),
                           reads=[f"pcol{i % 2}"], writes=[ckey])

        for jb in range(8):
            emit_mod_jb(jb)
        if stage == "mod":
            for jb in range(8, 24):
                emit_mod_jb(jb)

        if stage == "mod":
            sch.dma("sync", lambda s: s.dma_start(dbg[0:6, 0:2048], dmod),
                    reads=[f"dmod{v}" for v in range(6)])
            sch.wait_all_dmas("sync")
            sch.finalize(blk)
            return nc

        def ln_tile(src_ap, src_keys, lnxs_ap, lnxs_w):
            sch.op("vector", lambda v: v.reduce_sum(lns[:, 0:1], src_ap, axis=AX.X),
                   reads=src_keys, writes=["lns0"])
            sch.op("scalar", lambda s: s.activation(lnxs_ap, src_ap, AF.Square),
                   reads=src_keys, writes=lnxs_w())
            sch.op("vector", lambda v: v.reduce_sum(lns[:, 1:2], lnxs_ap, axis=AX.X),
                   reads=["lnxs"], writes=["lns1"])
            sch.op("vector", lambda v: v.tensor_scalar_mul(lns[:, 2:3], lns[:, 0:1], 1.0 / D),
                   reads=["lns0"], writes=["lns2"])
            sch.op("vector", lambda v: v.tensor_mul(lns[:, 3:4], lns[:, 2:3], lns[:, 2:3]),
                   reads=["lns2"], writes=["lns3"])
            sch.op("vector", lambda v: v.scalar_tensor_tensor(
                lns[:, 4:5], lns[:, 1:2], 1.0 / D, lns[:, 3:4], ALU.mult, ALU.subtract),
                reads=["lns1", "lns3"], writes=["lns4"])
            sch.op("vector", lambda v: v.tensor_scalar_add(lns[:, 4:5], lns[:, 4:5], EPS),
                   reads=["lns4"], writes=["lns4"])
            sch.op("scalar", lambda s: s.activation(lns[:, 5:6], lns[:, 4:5], AF.Sqrt),
                   reads=["lns4"], writes=["lns5"])
            sch.op("vector", lambda v: v.reciprocal(lns[:, 6:7], lns[:, 5:6]),
                   reads=["lns5"], writes=["lns6"])
            sch.op("vector", lambda v: v.scalar_tensor_tensor(
                lns[:, 7:8], lns[:, 2:3], -1.0, lns[:, 6:7], ALU.mult, ALU.mult),
                reads=["lns2", "lns6"], writes=["lns7"])
            sch.op("scalar", lambda s: s.activation(
                lnxs_ap, src_ap, AF.Identity, bias=lns[:, 7:8], scale=lns[:, 6:7]),
                reads=src_keys + ["lns6", "lns7"], writes=lnxs_w())

        def transpose_h_tile(t, lnxs_ap, ccol, scol_, ckeys):
            for ic in range(NIC):
                pcol = (ic % 2) * 512
                pv = pb0[:, pcol:pcol + 128]
                pk = f"ptr{ic % 2}"
                sch.op("tensor", lambda tt, ic=ic, pv=pv: tt.transpose(
                    pv, lnxs_ap[:, ic * 128:(ic + 1) * 128], identf[:]),
                    reads=["lnxs", "identf"],
                    writes=W(ar_p0, pk, pcol * 4, pcol * 4 + 2048))
                sch.op("vector", lambda v, ic=ic, pv=pv, t=t, ccol=ccol, scol_=scol_:
                       v.tensor_scalar(hT_tile(ic, t), pv, ccol[:, ic:ic + 1],
                                       scol_[:, ic:ic + 1], ALU.mult, ALU.add),
                       reads=[pk] + ckeys,
                       writes=W(ar_ht, f"hT{ic}_{t}",
                                ic * 2048 + t * 256, ic * 2048 + (t + 1) * 256))

        wsrc = [("wqt", 0), ("wqt", 1), ("wkt", 0), ("wkt", 1), ("wvt", 0), ("wvt", 1)]

        def emit_ln1_tile(sh, t, src_fn, src_keys_fn, lnxs_ap, lnxs_w, ccol, scol_, ckeys):
            if src_fn is not None:
                tg = sh * 8 + t
                sch.dma("sync", lambda s, tg=tg: s.dma_start(
                    xstage, dram["xin"][tg * 128:(tg + 1) * 128, :]),
                    writes=W(ar_x1, "xstage", 49152, 57344))
                src_ap, src_keys = xstage, ["xstage"]
            else:
                src_ap, src_keys = src_keys_fn
            ln_tile(src_ap, src_keys, lnxs_ap, lnxs_w)
            trb = pb0 if t % 2 == 0 else pb1
            trn = "pb0" if t % 2 == 0 else "pb1"
            tra = ar_p0 if t % 2 == 0 else ar_p1
            for ic in range(NIC):
                pcol = (ic % 2) * 512
                pv = trb[:, pcol:pcol + 128]
                pk = f"ptr{trn}{ic % 2}"
                sch.op("tensor", lambda tt, ic=ic, pv=pv, lnxs_ap=lnxs_ap: tt.transpose(
                    pv, lnxs_ap[:, ic * 128:(ic + 1) * 128], identf[:]),
                    reads=["lnxs", "identf"],
                    writes=W(tra, pk, pcol * 4, pcol * 4 + 2048))
                sch.op("vector", lambda v, ic=ic, pv=pv, t=t, ccol=ccol, scol_=scol_:
                       v.tensor_scalar(hT_tile(ic, t), pv, ccol[:, ic:ic + 1],
                                       scol_[:, ic:ic + 1], ALU.mult, ALU.add),
                       reads=[pk] + ckeys,
                       writes=W(ar_ht, f"hT{ic}_{t}",
                                ic * 2048 + t * 256, ic * 2048 + (t + 1) * 256))

        def emit_qkv_block(sh, ob):
            wname, wbi = wsrc[ob]
            wkey = f"awblk{ob % 2}"
            sch.dma("gpsimd", lambda g, wname=wname, wbi=wbi, ob=ob: g.dma_start(
                blkbuf(ob % 2)[:].rearrange("p (c n) -> p c n", c=NIC),
                dram[wname][wbi]),
                writes=W(ar_aw, wkey, (ob % 2) * 16384, (ob % 2 + 1) * 16384))
            isq, isv = ob < 2, ob >= 4
            bsrc = ["bq", "bq", "bk", "bk", "bv", "bv"][ob]
            brow = brows[0:1, (ob % 2) * 512:(ob % 2) * 512 + 512]
            brkey = f"brow{ob % 2}"
            sch.dma("gpsimd", lambda g, bsrc=bsrc, ob=ob, brow=brow: g.dma_start(
                brow, dram[bsrc][0:1, (ob % 2) * 512:(ob % 2) * 512 + 512]),
                writes=[brkey])
            for t in range(8):
                tg = sh * 8 + t
                pcol = (t % 2) * 512
                pv = pb0[:, pcol:pcol + 512]
                pk = f"pqkv{t % 2}"
                hkeys = [f"hT{ic}_{t}" for ic in range(NIC)]

                def qkvmm(tt, wkey=wkey, t=t, pv=pv, brow=brow, ob=ob):
                    for ic in range(NIC):
                        tt.matmul(pv, hT_tile(ic, t),
                                  blkbuf(ob % 2)[:, ic * 512:(ic + 1) * 512],
                                  start=(ic == 0), stop=False)
                    return tt.matmul(pv, onesr[:], brow, start=False, stop=True)
                sch.op("tensor", qkvmm, reads=[wkey, brkey, "onesr"] + hkeys,
                       writes=W(ar_p0, pk, pcol * 4, pcol * 4 + 2048))
                if isv:
                    dst = vb_t(t)[:, (ob - 4) * 512:(ob - 4) * 512 + 512]
                    sch.op("scalar", lambda s, pv=pv, dst=dst: s.copy(dst, pv),
                           reads=[pk],
                           writes=W(ar_x1, f"vb{sh}_{t}",
                                    32768 + t * 2048, 32768 + (t + 1) * 2048))
                else:
                    srb_ = srb if t % 2 == 0 else amsb[0:128, 14592:15104]
                    srbk = "srb" + ("A" if t % 2 == 0 else "B")
                    srb_lo = 19968 if t % 2 == 0 else 29184
                    sch.op("scalar", lambda s, pv=pv, srb_=srb_: s.copy(srb_[:], pv),
                           reads=[pk], writes=W(ar_ms, srbk, srb_lo, srb_lo + 1024))
                    dstbuf = qr_t(t) if isq else kr_t(t)
                    dkey = (f"qr{sh}_{t}" if isq else f"kr{sh}_{t}")
                    dlo = (0 if isq else 16384) + t * 2048
                    hoff = (ob % 2) * 512
                    srb3 = srb_[:].rearrange("p (h d) -> p h d", h=4)
                    dst3 = dstbuf[:, hoff:hoff + 512].rearrange("p (h d) -> p h d", h=4)
                    r13 = rtmp1[:, 0:256].rearrange("p (h d) -> p h d", h=4)
                    r23 = rtmp2[:, 0:256].rearrange("p (h d) -> p h d", h=4)
                    ca_ = cosbuf[:, tg * 64:(tg + 1) * 64].unsqueeze(1).broadcast_to([128, 4, 64])
                    sa_ = sinbuf[:, tg * 64:(tg + 1) * 64].unsqueeze(1).broadcast_to([128, 4, 64])
                    sch.op("vector", lambda v, r13=r13, srb3=srb3, ca_=ca_: v.tensor_mul(
                        r13, srb3[:, :, 0:64], ca_), reads=[srbk, "cosbuf"],
                        writes=W(ar_ms, "rtmp1", 14848, 15872))
                    sch.op("vector", lambda v, r23=r23, srb3=srb3, sa_=sa_: v.tensor_mul(
                        r23, srb3[:, :, 64:128], sa_), reads=[srbk, "sinbuf"],
                        writes=W(ar_ms, "rtmp2", 15872, 16896))
                    sch.op("vector", lambda v, dst3=dst3, r13=r13, r23=r23: v.tensor_sub(
                        dst3[:, :, 0:64], r13, r23), reads=["rtmp1", "rtmp2"],
                        writes=W(ar_x1, dkey, dlo, dlo + 2048))
                    sch.op("vector", lambda v, r13=r13, srb3=srb3, ca_=ca_: v.tensor_mul(
                        r13, srb3[:, :, 64:128], ca_), reads=[srbk, "cosbuf"],
                        writes=W(ar_ms, "rtmp1", 14848, 15872))
                    sch.op("vector", lambda v, r23=r23, srb3=srb3, sa_=sa_: v.tensor_mul(
                        r23, srb3[:, :, 0:64], sa_), reads=[srbk, "sinbuf"],
                        writes=W(ar_ms, "rtmp2", 15872, 16896))
                    sch.op("vector", lambda v, dst3=dst3, r13=r13, r23=r23: v.tensor_add(
                        dst3[:, :, 64:128], r13, r23), reads=["rtmp1", "rtmp2"],
                        writes=[dkey])

        def emit_mid_A(sh, t):
            par = t % 2
            qrk, krk = f"qr{sh}_{t}", f"kr{sh}_{t}"
            nqk, nkk = f"n2q{par}", f"n2k{par}"
            iqk, ikk = f"invq{par}", f"invk{par}"
            n2q_ = n2q[:, par * 8:par * 8 + 8]
            n2k_ = n2k[:, par * 8:par * 8 + 8]
            invq_ = invq[:, par * 8:par * 8 + 8]
            invk_ = invk[:, par * 8:par * 8 + 8]
            sch.op("vector", lambda v, t=t: v.tensor_mul(sqq, qr_t(t), qr_t(t)),
                   reads=[qrk], writes=W(ar_ms, "sqq", 20992, 23040))
            sch.op("vector", lambda v, n2q_=n2q_: v.reduce_sum(
                n2q_, sqq.rearrange("p (h d) -> p h d", h=8), axis=AX.X),
                reads=["sqq"], writes=[nqk])
            sch.op("vector", lambda v, t=t: v.tensor_mul(sqk, kr_t(t), kr_t(t)),
                   reads=[krk], writes=W(ar_ms, "sqk", 23040, 25088))
            sch.op("vector", lambda v, n2k_=n2k_: v.reduce_sum(
                n2k_, sqk.rearrange("p (h d) -> p h d", h=8), axis=AX.X),
                reads=["sqk"], writes=[nkk])
            for (n2_, inv_, nk, ik) in ((n2q_, invq_, nqk, iqk), (n2k_, invk_, nkk, ikk)):
                sch.op("scalar", lambda s, n2_=n2_, inv_=inv_: s.activation(
                    inv_, n2_, AF.Sqrt), reads=[nk], writes=[ik + "s"])
                sch.op("vector", lambda v, inv_=inv_: v.reciprocal(inv_, inv_),
                       reads=[ik + "s"], writes=[ik])
            for h in range(8):
                sch.op("vector", lambda v, t=t, h=h, invq_=invq_: v.tensor_scalar(
                    qr_t(t)[:, h * 128:(h + 1) * 128], qr_t(t)[:, h * 128:(h + 1) * 128],
                    invq_[:, h:h + 1], None, ALU.mult), reads=[qrk, iqk], writes=[qrk])
                sch.op("vector", lambda v, t=t, h=h, invk_=invk_: v.tensor_scalar(
                    kr_t(t)[:, h * 128:(h + 1) * 128], kr_t(t)[:, h * 128:(h + 1) * 128],
                    invk_[:, h:h + 1], None, ALU.mult), reads=[krk, ikk], writes=[krk])
            trb = pb0 if par == 0 else pb1
            trn = "pb0" if par == 0 else "pb1"
            tra = ar_p0 if par == 0 else ar_p1
            trbb = trb[:].bitcast(BF16)
            for (srcf, dstb, sk, dk, dlo, bank) in (
                    (qr_t(t), qnT, qrk, "qnT", 0, 0), (kr_t(t), knT, krk, "knT", 2048, 1)):
                pk = f"ptr{trn}{bank}"
                def trans8(tt, srcf=srcf, bank=bank, trbb=trbb):
                    for h in range(8):
                        ins = tt.transpose(trbb[:, bank * 1024 + h * 128:
                                                bank * 1024 + (h + 1) * 128],
                                           srcf[:, h * 128:(h + 1) * 128], identb[:])
                    return ins
                sch.op("tensor", trans8, reads=[sk, "identb"],
                       writes=W(tra, pk, bank * 2048, (bank + 1) * 2048))
                sch.op("vector", lambda v, dstb=dstb, bank=bank, trbb=trbb: v.tensor_copy(
                    dstb[:], trbb[:, bank * 1024:(bank + 1) * 1024]), reads=[pk],
                    writes=W(ar_ms, dk, dlo, dlo + 2048))
            for (srcT, sk, pcq, pk2) in ((qnT, "qnT", 1024, "prq" + trn),
                                         (knT, "knT", 1536, "prk" + trn)):
                def rmm(tt, srcT=srcT, pcq=pcq, trb=trb):
                    for h in range(8):
                        ins = tt.matmul(
                            trb[64 * (h % 2):64 * (h % 2) + 64,
                                pcq + (h // 2) * 128: pcq + (h // 2) * 128 + 128],
                            rtc[:], srcT[:, h * 128:(h + 1) * 128],
                            start=True, stop=True)
                    return ins
                sch.op("tensor", rmm, reads=[sk, "rtc"],
                       writes=W(tra, pk2, pcq * 4, pcq * 4 + 2048))

        def emit_mid_B(sh, t):
            par = t % 2
            vbk = f"vb{sh}_{t}"
            trb = pb0 if par == 0 else pb1
            trn = "pb0" if par == 0 else "pb1"
            tra = ar_p0 if par == 0 else ar_p1
            for (pcq, dstb, dk, dlo, upk, isq) in (
                    (1024, qparr, "qparr", 25088, "prq" + trn, True),
                    (1536, kparr, "kparr", 27136, "prk" + trn, False)):
                up = trb[:, pcq:pcq + 512]
                sch.op("scalar", lambda s, up=up: s.copy(ti32[:], up),
                       reads=[upk], writes=W(ar_ms, "ti32", 10752, 12800))
                sch.op("scalar", lambda s: s.copy(kfb[:], ti32[:]),
                       reads=["ti32"], writes=W(ar_ms, "kfb", 12800, 14848))
                sch.op("vector", lambda v, up=up: v.tensor_sub(wfr[:], up, kfb[:]),
                       reads=[upk, "kfb"], writes=W(ar_ms, "wfr", 14848, 16896))
                sch.op("scalar", lambda s: s.activation(s1b[:], wfr[:], AF.Sin,
                                                        scale=cpi[:, 0:1]),
                       reads=["wfr", "cpi"], writes=W(ar_ms, "s1b", 16896, 17920))
                sch.op("scalar", lambda s: s.activation(wabs[:], wfr[:], AF.Abs),
                       reads=["wfr"], writes=W(ar_ms, "wabs", 10752, 12800))
                sch.op("scalar", lambda s: s.activation(
                    c1b[:], wabs[:], AF.Sin, scale=cpi[:, 1:2], bias=cpi[:, 2:3]),
                    reads=["wabs", "cpi"], writes=W(ar_ms, "c1b", 17920, 18944))
                for half in range(2):
                    s1h = s1b[64 * half:64 * half + 64, :].rearrange(
                        "p (b g j) -> p b g j", b=4, g=8)
                    c1h = c1b[64 * half:64 * half + 64, :].rearrange(
                        "p (b g j) -> p b g j", b=4, g=8)
                    sq1h = sq1b[64 * half:64 * half + 64, :].rearrange(
                        "p (b g j) -> p b g j", b=4, g=8)
                    if isq:
                        dsin = dstb[0:64, :].rearrange(
                            "p (g hh j) -> p g hh j", g=8, hh=8)[:, :, half::2, :].transpose([0, 2, 1, 3])
                        dcos = dstb[64:128, :].rearrange(
                            "p (g hh j) -> p g hh j", g=8, hh=8)[:, :, half::2, :].transpose([0, 2, 1, 3])
                    else:
                        dsin = dstb[0:64, :].rearrange(
                            "p (g j e) -> p g j e", g=8, j=16)[:, :, :, half::2].transpose([0, 3, 1, 2])
                        dcos = dstb[64:128, :].rearrange(
                            "p (g j e) -> p g j e", g=8, j=16)[:, :, :, half::2].transpose([0, 3, 1, 2])
                    sch.op("vector", lambda v, s1h=s1h, c1h=c1h, dsin=dsin: v.tensor_mul(
                        dsin, s1h, c1h), reads=["s1b", "c1b"],
                        writes=W(ar_ms, dk, dlo, dlo + 2048))
                    sch.op("vector", lambda v, s1h=s1h, sq1h=sq1h: v.tensor_mul(
                        sq1h, s1h, s1h), reads=["s1b"],
                        writes=W(ar_ms, "sq1b", 18944, 19968))
                    sch.op("vector", lambda v, sq1h=sq1h, dcos=dcos: v.tensor_scalar(
                        dcos, sq1h, -1.0, 0.5, ALU.mult, ALU.add),
                        reads=["sq1b"], writes=[dk])
            for g in range(8):
                sch.dma("gpsimd", lambda gg, t=t, g=g: gg.dma_start(
                    v16[:, g * 128:(g + 1) * 128], vb_t(t)[g * 16:(g + 1) * 16, :]),
                    reads=[vbk],
                    writes=W(ar_ms, f"v16_{g}", 8192 + g * 256, 8192 + (g + 1) * 256))
            for g in range(8):
                pa = trb[:, (g % 2) * 512:(g % 2) * 512 + 128]
                pak = f"pa{trn}{g % 2}"
                sch.op("tensor", lambda tt, g=g, pa=pa: tt.matmul(
                    pa, kparr[:, g * 128:(g + 1) * 128], qparr[:, g * 128:(g + 1) * 128],
                    start=True, stop=True), reads=["kparr", "qparr"],
                    writes=W(tra, pak, (g % 2) * 2048, (g % 2) * 2048 + 2048))
                aslot = a16[g % 2]
                sch.op("vector", lambda v, pa=pa, aslot=aslot: v.tensor_mul(
                    aslot, pa, amaskb[:]), reads=[pak, "amaskb"],
                    writes=W(ar_ms, f"a16_{g % 2}", 10240 + (g % 2) * 256,
                             10240 + (g % 2 + 1) * 256))
                po = trb[:, 1024 + (g % 2) * 512:1024 + (g % 2) * 512 + 128]
                pok = f"po{trn}{g % 2}"
                sch.op("tensor", lambda tt, g=g, po=po, aslot=aslot: tt.matmul(
                    po, v16[:, g * 128:(g + 1) * 128], aslot, start=True, stop=True),
                    reads=[f"v16_{g}", f"a16_{g % 2}"],
                    writes=W(tra, pok, 4096 + (g % 2) * 2048, 4096 + (g % 2 + 1) * 2048))
                toff = sh * 1024 + t * 128 + g * 16
                sch.op("vector", lambda v, po=po, toff=toff: v.tensor_copy(
                    outT[:].rearrange("p (hl tt2) -> p hl tt2", hl=8)[:, :, toff:toff + 16],
                    po[:].rearrange("p (hl j) -> p hl j", hl=8)),
                    reads=[pok], writes=W(ar_ot, "outT", 0, 32768))

        for t in range(8):
            emit_ln1_tile(0, t, True, None, lnxsA,
                          lambda: W(ar_x1, "lnxs", 57344, 65536),
                          cacol, sacol, ["col1", "col0"])
        if stage == "ln1":
            sch.dma("sync", lambda s: s.dma_start(dbg[:, 0:8192], ahtf[:, 0:8192]),
                    reads=[f"hT{ic}_{t}" for ic in range(NIC) for t in range(8)])
            sch.wait_all_dmas("sync")
            sch.finalize(blk)
            return nc
        for ob in range(6):
            emit_qkv_block(0, ob)
        if stage == "qkv":
            sch.dma("sync", lambda s: s.dma_start(dbg[:, 0:12288], ax1f[:, 0:12288]),
                    reads=[f"qr0_{t}" for t in range(8)]
                    + [f"kr0_{t}" for t in range(8)]
                    + [f"vb0_{t}" for t in range(8)])
            sch.wait_all_dmas("sync")
            sch.finalize(blk)
            return nc
        for t in range(8):
            emit_mid_A(0, t)
            if t:
                emit_mid_B(0, t - 1)
            emit_ln1_tile(1, t, True, None, lnxsA,
                          lambda: W(ar_x1, "lnxs", 57344, 65536),
                          cacol, sacol, ["col1", "col0"])
        emit_mid_B(0, 7)
        for ob in range(6):
            emit_qkv_block(1, ob)
        for t in range(8):
            emit_mid_A(1, t)
            if t:
                emit_mid_B(1, t - 1)
            emit_mod_jb(8 + 2 * t)
            emit_mod_jb(9 + 2 * t)
        emit_mid_B(1, 7)

        if stage == "attn":
            sch.dma("sync", lambda s: s.dma_start(dbg[:, 0:8192], aotb[:].bitcast(F32)),
                    reads=["outT"])
            sch.wait_all_dmas("sync")
            sch.finalize(blk)
            return nc

        for hl in range(8):
            sch.op("vector", lambda v, hl=hl: v.tensor_copy(
                wo_buf[:, hl * 2048:(hl + 1) * 2048].rearrange("p (r q) -> p r q", r=16),
                outT[:, hl * 2048:(hl + 1) * 2048].rearrange("p (q r) -> p r q", q=128)),
                reads=["outT"], writes=W(ar_ms, f"wob{hl}", hl * 4096, (hl + 1) * 4096))
        for t in range(8):
            sch.dma("sync", lambda s, t=t: s.dma_start(
                x1t(t), dram["xown"][t * 128:(t + 1) * 128, :]),
                writes=W(ar_x1, f"x1_{t}", t * 8192, (t + 1) * 8192))

        sch.dma("gpsimd", lambda g: g.dma_start(gac, dmod[2:3, :].broadcast_to([128, D])),
                reads=["dmod2"], writes=W(ar_ht, "gac", 0, 8192))
        for ob in range(4):
            wkey = f"awblk{ob % 2}"
            sch.dma("gpsimd", lambda g, ob=ob: g.dma_start(
                blkbuf(ob % 2)[:].rearrange("p (c n) -> p c n", c=NIC), dram["wot"][ob]),
                writes=W(ar_aw, wkey, (ob % 2) * 16384, (ob % 2 + 1) * 16384))
            brow = brows[0:1, (ob % 2) * 512:(ob % 2) * 512 + 512]
            brkey = f"brow{ob % 2}"
            sch.dma("gpsimd", lambda g, ob=ob, brow=brow: g.dma_start(
                brow, dram["bo"][0:1, ob * 512:(ob + 1) * 512]), writes=[brkey])
            for t in range(8):
                pcol = (t % 2) * 512
                pv = pb0[:, pcol:pcol + 512]
                pk = f"pqkv{t % 2}"

                def womm(tt, t=t, ob=ob, pv=pv, brow=brow):
                    for r in range(16):
                        tt.matmul(pv, wo_buf[:, t * 2048 + r * 128: t * 2048 + (r + 1) * 128],
                                  blkbuf(ob % 2)[:, r * 512:(r + 1) * 512],
                                  start=(r == 0), stop=False)
                    return tt.matmul(pv, onesr[:], brow,
                                     start=False, stop=True)
                sch.op("tensor", womm, reads=[wkey, f"wob{t}", "onesr", brkey],
                       writes=W(ar_p0, pk, pcol * 4, pcol * 4 + 2048))
                ev = evt[t % 2]
                sch.op("vector", lambda v, pv=pv, ev=ev, ob=ob: v.tensor_mul(
                    ev, pv, gac[:, ob * 512:(ob + 1) * 512]),
                    reads=[pk, "gac"],
                    writes=W(ar_ht, f"evt{t % 2}", 8192 + (t % 2) * 2048,
                             8192 + (t % 2 + 1) * 2048))
                xsl = x1t(t)[:, ob * 512:(ob + 1) * 512]
                sch.op("vector", lambda v, xsl=xsl, ev=ev: v.tensor_add(xsl, xsl, ev),
                       reads=[f"evt{t % 2}", f"x1_{t}"], writes=[f"x1_{t}"])

        if stage == "wo":
            for t in range(8):
                sch.dma("sync", lambda s, t=t: s.dma_start(
                    dbg[:, t * 2048:(t + 1) * 2048], x1t(t)), reads=[f"x1_{t}"])
            sch.wait_all_dmas("sync")
            sch.finalize(blk)
            return nc

        for t in range(8):
            ln_tile(x1t(t), [f"x1_{t}"], lnxsB,
                    lambda: W(ar_ms, "lnxs", 16384, 24576))
            transpose_h_tile(t, lnxsB, cbcol, sbcol, ["col4", "col3"])

        sch.dma("gpsimd", lambda g: g.dma_start(awb[:, 0:128], dram["gwt"]),
                writes=W(ar_aw, "gwtb", 0, 256))
        for t in range(8):
            pg = pb1[:, 0:8]

            def gmm(tt, t=t, pg=pg):
                for ic in range(NIC):
                    ins = tt.matmul(pg, hT_tile(ic, t), awb[:, ic * 8:(ic + 1) * 8],
                                    start=(ic == 0), stop=(ic == NIC - 1))
                return ins
            sch.op("tensor", gmm, reads=["gwtb"] + [f"hT{ic}_{t}" for ic in range(NIC)],
                   writes=W(ar_p1, "pg", 0, 2048))
            gs = gsc[:, t * 8:(t + 1) * 8]
            sch.op("scalar", lambda s, pg=pg, gs=gs: s.activation(gs, pg, AF.Sigmoid),
                   reads=["pg"], writes=[f"gsc{t}"])
            bi = gtmp[:, 0:8]
            sch.op("vector", lambda v, gs=gs, bi=bi: v.tensor_add(bi, gs, ebc[:]),
                   reads=[f"gsc{t}", "ebc"], writes=["gbi"])
            sch.op("vector", lambda v, bi=bi: v.reduce_max(gm[:, 0:1], bi, axis=AX.X),
                   reads=["gbi"], writes=["gm0"])
            e1 = gtmp[:, 8:16]
            sch.op("vector", lambda v, bi=bi, e1=e1: v.tensor_scalar(
                e1, bi, gm[:, 0:1], None, ALU.is_ge), reads=["gbi", "gm0"], writes=["ge1"])
            msk = gtmp[:, 16:24]
            sch.op("vector", lambda v, e1=e1, bi=bi, msk=msk: v.scalar_tensor_tensor(
                msk, e1, 1e30, bi, ALU.mult, ALU.subtract),
                reads=["ge1", "gbi"], writes=["gmsk"])
            sch.op("vector", lambda v, msk=msk: v.tensor_reduce(
                gm[:, 1:2], msk, axis=AX.X, op=ALU.min), reads=["gmsk"], writes=["gm1"])
            e2 = gtmp[:, 24:32]
            sch.op("vector", lambda v, msk=msk, e2=e2: v.tensor_scalar(
                e2, msk, gm[:, 1:2], None, ALU.is_le), reads=["gmsk", "gm1"], writes=["ge2"])
            sch.op("vector", lambda v, e1=e1, e2=e2: v.tensor_add(gtmp[:, 0:8], e1, e2),
                   reads=["ge1", "ge2"], writes=["gmaskf"])
            sw_sl = sw_t[:, t * 8:(t + 1) * 8]
            sch.op("vector", lambda v, gs=gs, sw_sl=sw_sl: v.tensor_mul(
                sw_sl, gs, gtmp[:, 0:8]), reads=[f"gsc{t}", "gmaskf"], writes=[f"sw{t}"])
            pvt = pb1[0:8, 512:640]
            sch.op("tensor", lambda tt, sw_sl=sw_sl, pvt=pvt: tt.transpose(
                pvt, sw_sl, identf[:]), reads=[f"sw{t}", "identf"],
                writes=W(ar_p1, "pst", 2048, 4096))
            sch.op("vector", lambda v, t=t, pvt=pvt: v.tensor_copy(
                sT[:, t * 128:(t + 1) * 128], pvt), reads=["pst"],
                writes=W(ar_ms, "sT", 24576, 26624))
        sch.dma("gpsimd", lambda s: s.dma_start(sdram, sT[:]), reads=["sT"], writes=["sdram"])

        if stage == "gate":
            sch.dma("gpsimd", lambda g: g.dma_start(dbg[0:8, 0:1024], sdram), reads=["sdram"])
            sch.wait_all_dmas("sync")
            sch.finalize(blk)
            return nc

        sch.dma("gpsimd", lambda g: g.dma_start(gbc, dmod[5:6, :].broadcast_to([128, D])),
                reads=["dmod5"], writes=W(ar_ms, "gbc", 8192, 16384))
        pairs = [(0, 1), (2, 3), (4, 5), (6, 7), ("sh",)]
        h2keys = [f"hT{ic}_{t}" for ic in range(NIC) for t in range(8)]
        for pair in pairs:
            for ei, e in enumerate(pair):
                is_sh = (e == "sh")
                hebuf = heA if ei == 0 else heB
                hekey = f"he{ei}"
                sbkey = f"sbc{ei}"
                if not is_sh:
                    sch.dma("gpsimd", lambda g, e=e, ei=ei: g.dma_start(
                        sbc[ei], sdram[e:e + 1, :].broadcast_to([128, TOWN])),
                        reads=["sdram"],
                        writes=W(ar_ms, sbkey, ei * 4096, (ei + 1) * 4096))
                w1src = dram["sw1i"] if is_sh else dram["w1i"][e]
                w3src = dram["sw3i"] if is_sh else dram["w3i"][e]
                for hc in range(8):
                    qi = hc // 2
                    s1i = (qi % 2) * 2 + hc % 2
                    s3i = 4 + s1i
                    k1 = f"aws{(qi % 2) * 2}"
                    k3 = f"aws{4 + (qi % 2) * 2}"
                    if hc % 2 == 0:
                        s1lo = (qi % 2) * 2 * 4096
                        sch.dma("gpsimd", lambda g, w1src=w1src, hc=hc, s1lo=s1lo: g.dma_start(
                            awb[:, s1lo // 2:s1lo // 2 + 4096].rearrange(
                                "p (c q n) -> p c q n", c=2, q=NIC),
                            w1src[hc:hc + 2].rearrange("c p q n -> p c q n")),
                            writes=W(ar_aw, k1, s1lo, s1lo + 8192))
                        s3lo = 16384 + (qi % 2) * 2 * 4096
                        sch.dma("gpsimd", lambda g, w3src=w3src, hc=hc, s3lo=s3lo: g.dma_start(
                            awb[:, s3lo // 2:s3lo // 2 + 4096].rearrange(
                                "p (c q n) -> p c q n", c=2, q=NIC),
                            w3src[hc:hc + 2].rearrange("c p q n -> p c q n")),
                            writes=W(ar_aw, k3, s3lo, s3lo + 8192))
                    pu1 = pb0[:, (hc % 2) * 1024:(hc % 2) * 1024 + 1024]
                    pu3 = pb1[:, (hc % 2) * 1024:(hc % 2) * 1024 + 1024]
                    pu1k = f"pu1{hc % 2}"
                    pu3k = f"pu3{hc % 2}"

                    def umm(tt, sl, pu):
                        for ic in range(NIC):
                            for th in range(2):
                                ins = tt.matmul(
                                    pu[:, th * 512:(th + 1) * 512],
                                    slot(sl)[:, ic * 128:(ic + 1) * 128],
                                    ahtb[:, ic * 1024 + th * 512: ic * 1024 + (th + 1) * 512],
                                    start=(ic == 0), stop=(ic == NIC - 1))
                        return ins
                    sch.op("tensor", lambda tt, s1i=s1i, pu1=pu1: umm(tt, s1i, pu1),
                           reads=[k1] + h2keys,
                           writes=W(ar_p0, pu1k, (hc % 2) * 4096, (hc % 2 + 1) * 4096))
                    sch.op("tensor", lambda tt, s3i=s3i, pu3=pu3: umm(tt, s3i, pu3),
                           reads=[k3] + h2keys,
                           writes=W(ar_p1, pu3k, (hc % 2) * 4096, (hc % 2 + 1) * 4096))
                    hsl = hebuf[:, hc * 1024:(hc + 1) * 1024]
                    he_w = W(ar_ot, hekey, ei * 16384 + 0, ei * 16384 + 16384) if hc == 0 \
                        else [hekey]
                    if is_sh:
                        sch.op("scalar", lambda s, pu1=pu1: s.activation(scrB, pu1, AF.Silu),
                               reads=[pu1k], writes=W(ar_ms, "scrB", 20480, 24576))
                        sch.op("vector", lambda v, hsl=hsl, pu3=pu3: v.tensor_mul(
                            hsl, scrB, pu3), reads=["scrB", pu3k], writes=he_w)
                    else:
                        sch.op("vector", lambda v, pu1=pu1, ei=ei: v.tensor_mul(
                            scrA, pu1, sbc[ei]), reads=[pu1k, sbkey],
                            writes=W(ar_ms, "scrA", 16384, 20480))
                        sch.op("scalar", lambda s: s.activation(scrB, scrA, AF.Silu),
                               reads=["scrA"], writes=W(ar_ms, "scrB", 20480, 24576))
                        sch.op("vector", lambda v, pu3=pu3, ei=ei: v.tensor_mul(
                            scrA, pu3, sbc[ei]), reads=[pu3k, sbkey], writes=["scrA"])
                        sch.op("vector", lambda v, hsl=hsl: v.tensor_mul(hsl, scrB, scrA),
                               reads=["scrA", "scrB"], writes=he_w)
            w2s = [dram["sw2i"] if e == "sh" else dram["w2i"][e] for e in pair]
            w2ids = [8 if e == "sh" else e for e in pair]
            for t2 in range(4):
                taus = (2 * t2, 2 * t2 + 1)
                for ei, e in enumerate(pair):
                    hebuf = heA if ei == 0 else heB
                    hekey = f"he{ei}"
                    for hc in range(8):
                        hf = hc // 4
                        base = (ei * 2 + hf) % 2
                        ws = base * 4 + hc % 4
                        wkey = f"aws{base * 4}"
                        w2id = w2ids[ei]
                        if hc % 4 == 0:
                            lo = base * 4 * 4096
                            if t2 == 0:
                                sch.dma("gpsimd", lambda g, w2=w2s[ei], hc=hc, lo=lo:
                                        g.dma_start(
                                            awb[:, lo // 2:lo // 2 + 8192].rearrange(
                                                "p (c n) -> p c n", c=4),
                                            w2[hc:hc + 4].rearrange("c p n -> p c n")),
                                        writes=W(ar_aw, wkey, lo, lo + 16384))
                                sch.dma("scalar", lambda s, w2id=w2id, hc=hc, lo=lo:
                                        s.dma_start(
                                            w2scr[w2id, hc:hc + 4].rearrange("c p n -> p c n"),
                                            awb[:, lo // 2:lo // 2 + 8192].rearrange(
                                                "p (c n) -> p c n", c=4)),
                                        reads=[wkey], writes=[f"w2s{w2id}_{hf}"])
                            else:
                                sch.dma("sync", lambda s, w2id=w2id, hc=hc, lo=lo:
                                        s.dma_start(
                                            awb[:, lo // 2:lo // 2 + 8192].rearrange(
                                                "p (c n) -> p c n", c=4),
                                            w2scr[w2id, hc:hc + 4].rearrange("c p n -> p c n")),
                                        reads=[f"w2s{w2id}_{hf}"],
                                        writes=W(ar_aw, wkey, lo, lo + 16384))
                        start_f = (ei == 0 and hc == 0)
                        stop_f = (ei == len(pair) - 1 and hc == 7)
                        for tau in taus:
                            pw = pb0 if tau % 2 == 0 else pb1
                            pwk = f"pw{tau % 2}"
                            arp = ar_p0 if tau % 2 == 0 else ar_p1

                            def w2mm(tt, hebuf=hebuf, hc=hc, tau=tau, ws=ws, pw=pw,
                                     start_f=start_f, stop_f=stop_f):
                                for ob in range(4):
                                    ins = tt.matmul(
                                        pw[:, ob * 512:(ob + 1) * 512],
                                        hebuf[:, hc * 1024 + tau * 128:
                                              hc * 1024 + tau * 128 + 128],
                                        slot(ws)[:, ob * 512:(ob + 1) * 512],
                                        start=start_f, stop=stop_f)
                                return ins
                            sch.op("tensor", w2mm, reads=[wkey, hekey],
                                   writes=W(arp, pwk, 0, 8192))
                for tau in taus:
                    pw = pb0 if tau % 2 == 0 else pb1
                    pwk = f"pw{tau % 2}"
                    sch.op("vector", lambda v, pw=pw: v.tensor_mul(scr2, pw[:], gbc),
                           reads=[pwk, "gbc"], writes=W(ar_ms, "scr2", 24576, 32768))
                    sch.op("vector", lambda v, tau=tau: v.tensor_add(
                        x1t(tau), x1t(tau), scr2), reads=["scr2", f"x1_{tau}"],
                        writes=[f"x1_{tau}"])

        for t in range(8):
            sch.dma("sync", lambda s, t=t: s.dma_start(
                out[t * 128:(t + 1) * 128, :], x1t(t)), reads=[f"x1_{t}"])
        sch.wait_all_dmas("sync")
        sch.finalize(blk)
    return nc


def kernel(**inputs):
    in_maps = _prep(inputs)
    nc = _build(STAGE)
    res = run_bass_kernel_spmd(nc, in_maps, core_ids=list(range(8)))
    outb = np.empty((B, S, D), np.float32)
    for core in range(8):
        b, H = core // 2, core % 2
        outb[b, H * 1024:(H + 1) * 1024, :] = res.results[core]["out"]
    return outb
